# version 05
# baseline (speedup 1.0000x reference)
import numpy as np
import concourse.bass as bass
import concourse.mybir as mybir
from concourse.bass_utils import run_bass_kernel_spmd

F32 = mybir.dt.float32
BF16 = mybir.dt.bfloat16
AF = mybir.ActivationFunctionType
ALU = mybir.AluOpType
AX = mybir.AxisListType

T = 2048
D = 1024
TCH = 16
DCH = 8
LN_EPS = 1e-5

ENGINES = ("sync", "tensor", "vector", "scalar", "gpsimd")
SEMS = ("s_ld", "s_ldg", "s_pe", "s_act", "s_dve", "s_gp", "s_out")


class Prog:
    def __init__(self, nc):
        self.nc = nc
        self.q = {e: [] for e in ENGINES}
        self.sems = {}
        self.counts = {}
        self._sem_guards = []
        for s in SEMS:
            self._mk_sem(s)

    def new_sem(self, name):
        if name not in self.sems:
            self._mk_sem(name)
        return name

    def _mk_sem(self, name):
        guard = self.nc.semaphore(name)
        h = guard.__enter__()
        self._sem_guards.append(guard)
        self.sems[name] = h
        self.counts[name] = 0

    def emit(self, eng, fn):
        self.q[eng].append(fn)

    def wait(self, eng, sem, val=None):
        v = self.counts[sem] if val is None else val
        h = self.sems[sem]
        if v > 0:
            self.q[eng].append(lambda E, h=h, v=v: E.wait_ge(h, v))

    def inc(self, eng, sem, n, fn):
        h = self.sems[sem]
        self.counts[sem] += n
        self.q[eng].append(lambda E, fn=fn, h=h, n=n: fn(E).then_inc(h, n))
        return self.counts[sem]

    def barrier(self, engs=ENGINES):
        for e in engs:
            for s in SEMS:
                self.wait(e, s)

    def build(self):
        with self.nc.Block() as block:
            for e in ENGINES:
                items = self.q[e]
                if not items:
                    continue

                def body(E, items=items):
                    for fn in items:
                        fn(E)
                getattr(block, e)(body)
        for g in reversed(self._sem_guards):
            g.__exit__(None, None, None)


def emit_ln(p, x_sb, xln_fm, B, fp32_rows=None):
    bn6, mv, sd, rstd = B["bn6"], B["mv"], B["sd"], B["rstd"]
    xln_t, eye_bf = B["xln_t"], B["eye_bf"]

    for c in range(TCH):
        xc = x_sb[:, c, :]
        p.emit("vector", lambda E, xc=xc: E.bn_stats(bn6[:, 0, :], xc[:, 0:512]))
        p.emit("vector", lambda E, xc=xc: E.bn_stats(bn6[:, 1, :], xc[:, 512:1024]))
        p.emit("vector", lambda E: E.drain())
        p.emit("vector", lambda E: E.bn_aggr(mv, bn6))
        p.emit("vector", lambda E: E.drain())
        p.inc("vector", "s_dve", 1,
              lambda E: E.tensor_scalar(mv[:, 1:2], mv[:, 1:2], LN_EPS, None, ALU.add))
        p.wait("scalar", "s_dve")
        p.inc("scalar", "s_act", 1, lambda E: E.activation(sd, mv[:, 1:2], AF.Sqrt))
        p.wait("vector", "s_act")
        p.emit("vector", lambda E: E.reciprocal(rstd, sd))
        p.emit("vector", lambda E: E.drain())
        p.inc("vector", "s_dve", 1,
              lambda E, xc=xc:
              E.tensor_scalar(xln_t, xc, mv[:, 0:1], rstd, ALU.subtract, ALU.mult))
        if fp32_rows is not None:
            p.inc("vector", "s_dve", 1,
                  lambda E, xc=xc, o=fp32_rows[:, c, :]:
                  E.tensor_scalar(o, xc, mv[:, 0:1], rstd, ALU.subtract, ALU.mult))
        tr_evac = {}
        for j in range(DCH):
            if j < 2:
                p.wait("tensor", "s_dve")
            else:
                p.wait("tensor", "s_dve", tr_evac[j - 2])
            ck = p.inc("tensor", "s_pe", 1,
                       lambda E, j=j, b=B["ps_t0" if j % 2 == 0 else "ps_t1"]:
                       E.transpose(b, xln_t[:, j * 128:(j + 1) * 128], eye_bf))
            p.wait("vector", "s_pe", ck)
            tr_evac[j] = p.inc("vector", "s_dve", 1,
                               lambda E, c=c, j=j, b=B["ps_t0" if j % 2 == 0 else "ps_t1"]:
                               E.tensor_copy(xln_fm[:, j, c * 128:(c + 1) * 128], b))


def emit_l0_attn(p, x_sb, xln_fm, B, kt_d, v_d, stage=3, extra_work=None):
    SCH = 32
    bn6, mv, sd, rstd = B["bn6"], B["mv"], B["sd"], B["rstd"]
    xln_t = B["xln_t"]

    def emit_ln_quarter(q):
        dst = xln_fm
        p.wait("tensor", "s_pe")
        for ci in range(4):
            c = q * 4 + ci
            xc = x_sb[:, c, :]
            p.emit("vector", lambda E, xc=xc: E.bn_stats(bn6[:, 0, :], xc[:, 0:512]))
            p.emit("vector", lambda E, xc=xc: E.bn_stats(bn6[:, 1, :], xc[:, 512:1024]))
            p.emit("vector", lambda E: E.drain())
            p.emit("vector", lambda E: E.bn_aggr(mv, bn6))
            p.emit("vector", lambda E: E.drain())
            p.inc("vector", "s_dve", 1,
                  lambda E: E.tensor_scalar(mv[:, 1:2], mv[:, 1:2], LN_EPS, None, ALU.add))
            p.wait("scalar", "s_dve")
            p.inc("scalar", "s_act", 1, lambda E: E.activation(sd, mv[:, 1:2], AF.Sqrt))
            p.wait("vector", "s_act")
            p.emit("vector", lambda E: E.reciprocal(rstd, sd))
            p.emit("vector", lambda E: E.drain())
            p.inc("vector", "s_dve", 1,
                  lambda E, xc=xc:
                  E.tensor_scalar(xln_t, xc, mv[:, 0:1], rstd, ALU.subtract, ALU.mult))
            emit_tm_to_fm(p, xln_t,
                          [(j, dst[:, j, ci * 128:(ci + 1) * 128]) for j in range(DCH)],
                          B)
        return p.counts["s_dve"]
    kt_sb = [B["kt0"], B["kt1"]]
    g_fm = B["g_fm"]
    g2 = [B["g20"], B["g21"]]
    v_sb = B["v_sb"]
    ps_ab = [B["ps_a"], B["ps_b"]]
    ps_cd = [B["ps_c"], B["ps_d"]]
    ps_row = B["ps_row"]
    ps_tcol = B["ps_tcol"]
    ones_bf = B["ones_bf"]
    ones_f32 = B["ones_f32"]
    ssrow = B["ssrow"]
    tmp4 = B["tmp4"]
    r64 = B["r64"]

    kt4 = kt_d.rearrange("(c p) s -> p c s", p=128)
    v4 = v_d.rearrange("(c p) d -> p c d", p=128)
    p.new_sem("kt_p0")
    p.new_sem("kt_p1")
    p.new_sem("sv")

    prev_last_mm = None
    prev_gelu_last = None
    prev_ss_copy = None
    prev_bev = {}
    first_ss_bank_touch = [True]
    n_extra = len(extra_work) if extra_work else 0
    ln_ck = {}
    ln_ck[0] = emit_ln_quarter(0)

    import os as _os
    for i in range(4 if _os.environ.get("NOV") != "1" else 0):
        p.inc("gpsimd", "sv", 16,
              lambda E, i=i:
              E.dma_start(out=v_sb[:, i * 8:(i + 1) * 8, :],
                          in_=v4[:, i * 8:(i + 1) * 8, :]))
    sv_total = p.counts["sv"]

    for q in range(4):
        xq_cur = xln_fm
        if extra_work:
            k0 = (n_extra * q) // 4
            k1 = (n_extra * (q + 1)) // 4
            for w in extra_work[k0:k1]:
                w()
        p.wait("tensor", "s_dve", ln_ck[q])
        ck_mm = {}
        ck_ss = {}
        ck_gelu = {}
        ck_sq = {}

        def emit_ss(s, q=q, ck_sq=ck_sq, ck_ss=ck_ss):
            p.wait("tensor", "s_dve", ck_sq[s])
            if s == 0 and prev_ss_copy is not None:
                p.wait("tensor", "s_dve", prev_ss_copy)
            ck_ss[s] = p.inc("tensor", "s_pe", 1,
                             lambda E, s=s:
                             E.matmul(ps_row, ones_bf, g2[s % 2],
                                      start=(s == 0), stop=(s == SCH - 1)))

        for s in range(SCH):
            buf = s % 2
            if (s - 2) in ck_mm:
                p.wait("gpsimd", "s_pe", ck_mm[s - 2])
            elif prev_last_mm is not None:
                p.wait("gpsimd", "s_pe", prev_last_mm)
            ktsem = "kt_p%d" % buf
            ck_kt = p.inc("gpsimd", ktsem, 16,
                          lambda E, s=s, buf=buf:
                          E.dma_start(out=kt_sb[buf], in_=kt4[:, :, s * 128:(s + 1) * 128]))
            p.wait("tensor", ktsem, ck_kt)
            if _os.environ.get("NOMM") == "1":
                ck_mm[s] = p.counts["s_pe"]
                ck_gelu[s - 0] = ck_gelu.get(s)
                ck_sq[s] = p.counts["s_act"]
                continue
            if (s - 2) in ck_gelu:
                p.wait("tensor", "s_act", ck_gelu[s - 2])
            elif prev_gelu_last is not None:
                p.wait("tensor", "s_act", prev_gelu_last)
            if s < 2 and (6 + s) in prev_bev:
                p.wait("tensor", "s_dve", prev_bev[6 + s])
            for k in range(DCH - 1):
                p.emit("tensor",
                       lambda E, s=s, k=k, buf=buf, xq_cur=xq_cur:
                       E.matmul(ps_ab[s % 2], kt_sb[buf][:, k, :],
                                xq_cur[:, k, :],
                                start=(k == 0), stop=False))
            ck_mm[s] = p.inc("tensor", "s_pe", 1,
                             lambda E, s=s, buf=buf, xq_cur=xq_cur:
                             E.matmul(ps_ab[s % 2], kt_sb[buf][:, DCH - 1, :],
                                      xq_cur[:, DCH - 1, :],
                                      start=False, stop=True))
            if s >= 1 and stage >= 2:
                emit_ss(s - 1)
            if _os.environ.get("NOGELU") == "1":
                ck_sq[s] = p.counts["s_act"]
                continue
            p.wait("scalar", "s_pe", ck_mm[s])
            ck_gelu[s] = p.inc(
                "scalar", "s_act", 1,
                lambda E, s=s: E.activation(g_fm[:, s, :], ps_ab[s % 2], AF.Gelu))
            if stage < 2:
                ck_sq[s] = p.counts["s_dve"]
                continue
            p.wait("vector", "s_act", ck_gelu[s])
            if (s - 2) in ck_ss:
                p.wait("vector", "s_pe", ck_ss[s - 2])
            ck_sq[s] = p.inc("vector", "s_dve", 1,
                             lambda E, s=s:
                             E.tensor_tensor(g2[s % 2], g_fm[:, s, :], g_fm[:, s, :],
                                             ALU.mult))
        if stage >= 2:
            emit_ss(SCH - 1)
        prev_last_mm = ck_mm[SCH - 1]
        if ck_gelu:
            prev_gelu_last = ck_gelu[SCH - 1]
        if q + 1 < 4:
            ln_ck[q + 1] = emit_ln_quarter(q + 1)
        if stage < 2:
            continue

        p.wait("vector", "s_pe", ck_ss[SCH - 1])
        p.emit("vector", lambda E: E.tensor_copy(ssrow, ps_row))
        prev_ss_copy = p.inc("vector", "s_dve", 1, lambda E: E.drain())
        p.wait("tensor", "s_dve", prev_ss_copy)
        for j in range(3):
            p.emit("tensor",
                   lambda E, j=j:
                   E.matmul(ps_tcol[:, j:j + 1], ssrow[0:1, j * 128:(j + 1) * 128],
                            ones_f32[0:1, 0:1], start=(j == 0), stop=False))
        ck_tc = p.inc("tensor", "s_pe", 1,
                      lambda E:
                      E.matmul(ps_tcol[:, 3:4], ssrow[0:1, 384:512],
                               ones_f32[0:1, 0:1], start=False, stop=True))
        p.wait("vector", "s_pe", ck_tc)
        p.emit("vector", lambda E: E.tensor_copy(tmp4, ps_tcol))
        p.inc("vector", "s_dve", 1, lambda E: E.drain())
        p.wait("scalar", "s_dve")
        p.inc("scalar", "s_act", 1,
              lambda E: E.activation(tmp4, tmp4, AF.Sqrt, scale=1.0 / 4096.0))
        p.wait("vector", "s_act")
        p.emit("vector", lambda E, q=q: E.reciprocal(r64[:, q * 4:(q + 1) * 4], tmp4))
        r64_done = p.inc("vector", "s_dve", 1, lambda E: E.drain())

        ck_bev = {}
        if stage < 3:
            continue
        for j in range(4):
            c = q * 4 + j
            pair = [ps_ab, ps_cd][j % 2]
            if j >= 2 and (j - 2) in ck_bev:
                p.wait("tensor", "s_dve", ck_bev[j - 2])
            elif j == 0:
                p.wait("tensor", "s_act", ck_gelu[SCH - 1])
            elif j == 1 and (3 in prev_bev):
                p.wait("tensor", "s_dve", prev_bev[3])
            if q == 0 and j == 0:
                p.wait("tensor", "sv", sv_total)
            for s in range(SCH):
                for dv in range(2):
                    if s == SCH - 1 and dv == 1:
                        continue
                    p.emit("tensor",
                           lambda E, s=s, dv=dv, j=j, pair=pair:
                           E.matmul(pair[dv], g_fm[:, s, j * 128:(j + 1) * 128],
                                    v_sb[:, s, dv * 512:(dv + 1) * 512],
                                    start=(s == 0), stop=(s == SCH - 1)))
            ck_bmm = p.inc("tensor", "s_pe", 1,
                           lambda E, j=j, pair=pair:
                           E.matmul(pair[1], g_fm[:, SCH - 1, j * 128:(j + 1) * 128],
                                    v_sb[:, SCH - 1, 512:1024],
                                    start=False, stop=True))
            p.wait("vector", "s_pe", ck_bmm)
            if j == 0:
                p.wait("vector", "s_dve", r64_done)
            p.emit("vector",
                   lambda E, c=c, pair=pair:
                   E.scalar_tensor_tensor(x_sb[:, c, 0:512], pair[0], r64[:, c:c + 1],
                                          x_sb[:, c, 0:512], ALU.mult, ALU.add))
            ck_bev[j] = p.inc("vector", "s_dve", 1,
                              lambda E, c=c, pair=pair:
                              E.scalar_tensor_tensor(x_sb[:, c, 512:1024], pair[1],
                                                     r64[:, c:c + 1],
                                                     x_sb[:, c, 512:1024],
                                                     ALU.mult, ALU.add))
        prev_bev = {3: ck_bev[3], 6: ck_bev[2], 7: ck_bev[3]}


def emit_tm_to_fm(p, src_tm, dst_fm_writes, B):
    tr_evac = {}
    p.wait("tensor", "s_gp")
    for idx, (j, dst) in enumerate(dst_fm_writes):
        b = B["ps_t0"] if idx % 2 == 0 else B["ps_t1"]
        if idx < 2:
            p.wait("tensor", "s_dve")
        else:
            p.wait("tensor", "s_dve", tr_evac[idx - 2])
        ck = p.inc("tensor", "s_pe", 1,
                   lambda E, j=j, b=b, src_tm=src_tm:
                   E.transpose(b, src_tm[:, j * 128:(j + 1) * 128], B["eye_bf"]))
        p.wait("vector", "s_pe", ck)
        tr_evac[idx] = p.inc("vector", "s_dve", 1,
                             lambda E, dst=dst, b=b: E.tensor_copy(dst, b))
    return tr_evac[len(dst_fm_writes) - 1]


QTHR = (0.25, 0.75, 1.25, 1.75, 2.5, 3.5, 5.0)


def emit_quant(p, src32, dst_bf, N, Q, eng="vector"):
    nb = N // 16
    xn, A, Bb, sgn = Q["xn"], Q["qa"], Q["qb"], Q["sgn"]
    sc, rs, hs = Q["sc"], Q["rs"], Q["hs"]
    esem = "s_dve" if eng == "vector" else "s_gp"
    s3 = lambda ap: ap[:, 0:N].rearrange("p (g k) -> p g k", k=16)
    scb = lambda ap: ap[:, 0:nb].rearrange("p (g o) -> p g o", o=1).broadcast_to([128, nb, 16])
    _V = lambda fn: p.emit(eng, fn)
    D = lambda: p.emit(eng, lambda E: E.drain())

    def V(fn):
        D()
        _V(fn)

    if eng != "vector":
        p.wait(eng, "s_pe")

    p.emit("vector", lambda E: E.drain())
    p.emit("vector", lambda E: E.tensor_reduce(
        sc[:, 0:nb].rearrange("p (g o) -> p g o", o=1),
        s3(src32), AX.X, ALU.max, apply_absolute_value=True))
    p.emit("vector", lambda E: E.drain())
    p.emit("vector", lambda E: E.tensor_scalar(sc[:, 0:nb], sc[:, 0:nb], 1.0 / 6.0,
                                               1e-12, ALU.mult, ALU.max))
    p.emit("vector", lambda E: E.drain())
    p.emit("vector", lambda E: E.tensor_scalar(hs[:, 0:nb], sc[:, 0:nb], 0.5, None,
                                               ALU.mult))
    p.emit("vector", lambda E: E.reciprocal(rs[:, 0:nb], sc[:, 0:nb]))
    ck_rs = p.inc("vector", "s_dve", 1, lambda E: E.drain())
    if eng != "vector":
        p.wait(eng, "s_dve", ck_rs)
    p.wait(eng, "s_act")
    ck_xn = p.inc(eng, esem, 1,
                  lambda E: E.tensor_tensor(s3(xn), s3(src32), scb(rs), ALU.mult))
    D()
    p.wait("scalar", esem, ck_xn)
    ck_sgn = p.inc("scalar", "s_act", 1,
                   lambda E: E.activation(sgn[:, 0:N], xn[:, 0:N], AF.Sign))
    ax = Q["ax"]
    U32 = mybir.dt.uint32
    if eng == "vector":
        V(lambda E: E.tensor_scalar(ax[:, 0:N].bitcast(U32), xn[:, 0:N].bitcast(U32),
                                    0x7FFFFFFF, None, ALU.bitwise_and))
    else:
        p.wait(eng, "s_act", ck_sgn)
        V(lambda E: E.tensor_tensor(ax[:, 0:N], xn[:, 0:N], sgn[:, 0:N], ALU.mult))
    cmp = lambda dst, th: V(lambda E, dst=dst, th=th: E.tensor_scalar(
        dst[:, 0:N], ax[:, 0:N], th, None, ALU.is_gt))
    cmp(A, QTHR[0])
    for th in (QTHR[1], QTHR[2], QTHR[3]):
        cmp(Bb, th)
        D()
        V(lambda E: E.tensor_tensor(A[:, 0:N], A[:, 0:N], Bb[:, 0:N], ALU.add))
    for th, w in ((QTHR[4], 2.0), (QTHR[5], 2.0), (QTHR[6], 4.0)):
        cmp(Bb, th)
        D()
        if eng == "vector":
            V(lambda E, w=w: E.scalar_tensor_tensor(A[:, 0:N], Bb[:, 0:N], w,
                                                    A[:, 0:N], ALU.mult, ALU.add))
        else:
            V(lambda E, w=w: E.tensor_scalar(Bb[:, 0:N], Bb[:, 0:N], w, None, ALU.mult))
            V(lambda E: E.tensor_tensor(A[:, 0:N], A[:, 0:N], Bb[:, 0:N], ALU.add))
    D()
    p.wait(eng, "s_act", ck_sgn)
    V(lambda E: E.tensor_tensor(A[:, 0:N], A[:, 0:N], sgn[:, 0:N], ALU.mult))
    D()
    ck = p.inc(eng, esem, 1,
               lambda E: E.tensor_tensor(s3(dst_bf), s3(A), scb(hs), ALU.mult))
    return (esem, ck)




def emit_wq_tile(p, B, raw_src_ap, wq_dram_dst_ap, stage_buf, ck_stage_sem, ck_stage_val):
    Qw = B["Qw"]
    p.wait("vector", ck_stage_sem, ck_stage_val)
    p.wait("gpsimd", ck_stage_sem, ck_stage_val)
    emit_quant(p, stage_buf, Qw["dst_tm"], 1024, Qw, eng="gpsimd")
    wqc = B["wqc"]
    p.wait("vector", "swq")
    last = emit_tm_to_fm(p, Qw["dst_tm"],
                         [(j, wqc[:, j, :]) for j in range(DCH)], B)
    p.wait("sync", "s_dve", last)
    return p.inc("sync", "swq", 16,
                 lambda E, dst=wq_dram_dst_ap: E.dma_start(out=dst, in_=wqc))


FFN_I = 2816
FFN_STRIPS = [(0, 512), (512, 512), (1024, 512), (1536, 512), (2048, 512), (2560, 256)]


def emit_l0_ffn(p, x_sb, B, wgt_q, wut_q, wd_d):
    bn6, mv, sd, rstd = B["bn6"], B["mv"], B["sd"], B["rstd"]
    QQ = [B["Qv"], B["Qg"]]
    ENG = ["vector", "gpsimd"]
    xq_fm = B["xq_fm"]
    wd_raw = B["wd_raw"]
    wgT, wuT, wdT = B["wgT"], B["wuT"], B["wdT"]
    sg = B["sg"]
    h32 = B["h32"]
    hq_fm = B["hq_fm"]
    ps_g, ps_u = B["ps_a"], B["ps_b"]
    ps_d0, ps_d1 = B["ps_c"], B["ps_d"]
    p.new_sem("sw")
    p.new_sem("swgld")

    wd4 = wd_d.rearrange("(h p) i -> p h i", p=128)

    qck = {}
    for c in range(TCH):
        Qc, eng = QQ[c % 2], ENG[c % 2]
        xc = x_sb[:, c, :]
        p.emit("vector", lambda E, xc=xc: E.bn_stats(bn6[:, 0, :], xc[:, 0:512]))
        p.emit("vector", lambda E, xc=xc: E.bn_stats(bn6[:, 1, :], xc[:, 512:1024]))
        p.emit("vector", lambda E: E.drain())
        p.emit("vector", lambda E: E.bn_aggr(mv, bn6))
        p.emit("vector", lambda E: E.drain())
        p.inc("vector", "s_dve", 1,
              lambda E: E.tensor_scalar(mv[:, 1:2], mv[:, 1:2], LN_EPS, None, ALU.add))
        p.wait("scalar", "s_dve")
        p.inc("scalar", "s_act", 1, lambda E: E.activation(sd, mv[:, 1:2], AF.Sqrt))
        p.wait("vector", "s_act")
        p.emit("vector", lambda E: E.reciprocal(rstd, sd))
        p.emit("vector", lambda E: E.drain())
        if (c - 2) in qck:
            p.wait("vector", qck[c - 2][0], qck[c - 2][1])
        p.inc("vector", "s_dve", 1,
              lambda E, xc=xc, Qc=Qc:
              E.tensor_scalar(Qc["x32"], xc, mv[:, 0:1], rstd, ALU.subtract, ALU.mult))
        qck[c] = emit_quant(p, Qc["x32"], Qc["dst_tm"], 1024, Qc, eng=eng)
        emit_tm_to_fm(p, Qc["dst_tm"],
                      [(j, xq_fm[:, j, c * 128:(c + 1) * 128]) for j in range(DCH)], B)

    prev_w_ready = None
    prev_gu_mm = [None]
    for (i0, W) in FFN_STRIPS:
        nt = W // 128
        r0 = i0 // 128
        if prev_w_ready is not None:
            p.wait("gpsimd", "s_dve", prev_w_ready)
        sw_tot = p.inc("gpsimd", "sw", 16,
                       lambda E, i0=i0, W=W: E.dma_start(out=wd_raw[:, :, 0:W],
                                                         in_=wd4[:, :, i0:i0 + W]))
        if prev_gu_mm[0] is not None:
            p.wait("sync", "s_pe", prev_gu_mm[0])
        p.wait("sync", "swq")
        p.inc("sync", "swgld", 16,
              lambda E, i0=i0, W=W: E.dma_start(
                  out=wgT[:, :, 0:W],
                  in_=wgt_q[:, :, i0:i0 + W].rearrange("kc p i -> p kc i")))
        gu_ld = p.inc("sync", "swgld", 16,
                      lambda E, i0=i0, W=W: E.dma_start(
                          out=wuT[:, :, 0:W],
                          in_=wut_q[:, :, i0:i0 + W].rearrange("kc p i -> p kc i")))
        p.wait("vector", "sw", sw_tot)
        wdck = {}
        for ht in range(8):
            Qc, eng = QQ[ht % 2], ENG[ht % 2]
            if (ht - 2) in wdck:
                pass
            wdck[ht] = emit_quant(p, wd_raw[:, ht, 0:W], Qc["dst_tm"][:, 0:W], W, Qc,
                                  eng=eng)
            emit_tm_to_fm(p, Qc["dst_tm"][:, 0:W],
                          [(ic, wdT[:, ic, ht * 128:(ht + 1) * 128]) for ic in range(nt)],
                          B)
        w_ready = p.counts["s_dve"]
        p.wait("tensor", "s_gp")
        prev_w_ready = w_ready

        ck_evac = {}
        hck = {}
        for c in range(TCH):
            if c == 0:
                p.wait("tensor", "s_dve", w_ready)
                p.wait("tensor", "swgld", gu_ld)
            p.wait("tensor", "s_act")
            for k in range(DCH - 1):
                p.emit("tensor",
                       lambda E, k=k, c=c, W=W:
                       E.matmul(ps_g[:, 0:W], xq_fm[:, k, c * 128:(c + 1) * 128],
                                wgT[:, k, 0:W], start=(k == 0), stop=False))
            ck_g = p.inc("tensor", "s_pe", 1,
                         lambda E, c=c, W=W:
                         E.matmul(ps_g[:, 0:W], xq_fm[:, DCH - 1, c * 128:(c + 1) * 128],
                                  wgT[:, DCH - 1, 0:W], start=False, stop=True))
            for k in range(DCH - 1):
                p.emit("tensor",
                       lambda E, k=k, c=c, W=W:
                       E.matmul(ps_u[:, 0:W], xq_fm[:, k, c * 128:(c + 1) * 128],
                                wuT[:, k, 0:W], start=(k == 0), stop=False))
            ck_u = p.inc("tensor", "s_pe", 1,
                         lambda E, c=c, W=W:
                         E.matmul(ps_u[:, 0:W], xq_fm[:, DCH - 1, c * 128:(c + 1) * 128],
                                  wuT[:, DCH - 1, 0:W], start=False, stop=True))
            Qc, eng = QQ[c % 2], ENG[c % 2]
            p.wait("scalar", "s_pe", ck_g)
            p.wait("scalar", "s_dve")
            ck_silu = p.inc("scalar", "s_act", 1,
                            lambda E, W=W, c=c:
                            E.activation(sg[:, c % 2, 0:W], ps_g[:, 0:W], AF.Silu))
            p.wait("vector", "s_act", ck_silu)
            p.wait("vector", "s_pe", ck_u)
            if (c - 2) in hck:
                p.wait("vector", hck[c - 2][0], hck[c - 2][1])
            p.inc("vector", "s_dve", 1,
                  lambda E, W=W, c=c:
                  E.tensor_tensor(h32[:, c % 2, 0:W], sg[:, c % 2, 0:W], ps_u[:, 0:W],
                                  ALU.mult))
            hck[c] = emit_quant(p, h32[:, c % 2, 0:W], Qc["dst_tm"][:, 0:W], W, Qc,
                                eng=eng)
            hq_ready = emit_tm_to_fm(p, Qc["dst_tm"][:, 0:W],
                                     [(ic, hq_fm[:, ic, :]) for ic in range(nt)], B)
            p.wait("tensor", "s_dve", hq_ready)
            for hh, psd in ((0, ps_d0), (1, ps_d1)):
                if c >= 1:
                    p.wait("tensor", "s_dve", ck_evac[(c - 1, hh)])
                for ic in range(nt - 1):
                    p.emit("tensor",
                           lambda E, ic=ic, hh=hh:
                           E.matmul(psd if False else (ps_d0 if hh == 0 else ps_d1),
                                    hq_fm[:, ic, :], wdT[:, ic, hh * 512:(hh + 1) * 512],
                                    start=(ic == 0), stop=False))
                ck_d = p.inc("tensor", "s_pe", 1,
                             lambda E, hh=hh, nt=nt:
                             E.matmul(ps_d0 if hh == 0 else ps_d1,
                                      hq_fm[:, nt - 1, :],
                                      wdT[:, nt - 1, hh * 512:(hh + 1) * 512],
                                      start=(nt == 1), stop=True))
                p.wait("vector", "s_pe", ck_d)
                ck_evac[(c, hh)] = p.inc(
                    "vector", "s_dve", 1,
                    lambda E, c=c, hh=hh:
                    E.tensor_tensor(x_sb[:, c, hh * 512:(hh + 1) * 512],
                                    x_sb[:, c, hh * 512:(hh + 1) * 512],
                                    ps_d0 if hh == 0 else ps_d1, ALU.add))
        prev_gu_mm[0] = p.counts["s_pe"]


def emit_l1_attn(p, x_sb, B, wqkv_d, wo_d, skip_cc=False):
    nc = p.nc
    Q = B["Q"]
    bn6, mv, sd, rstd = B["bn6"], B["mv"], B["sd"], B["rstd"]
    xln_t = B["xln_t"]
    xr = B["xr"]
    q_ch = B["q_ch"]
    q_ld = [B["q_ld0"], B["q_ld1"]]
    q_dram = nc.dram_tensor("q_spill", [TCH, 128, 1024], BF16, kind="Internal").ap()
    kn_tm = B["kn_tm"]
    v_tm = B["v_tm"]
    y_fm = B["y_fm"]
    wt = [B["wt0"], B["wt1"]]
    ks32 = Q["xn"]
    sq32 = B["sq_bf"]
    st16 = B["st16"]
    st16b = B["st16b"]
    q_rot = B["q_rot"]
    kn_rot = B["kn_rot"]
    scm = [B["scm0"], B["scm1"]]
    ST32 = B["ST32"]
    STbf = B["STbf"]
    T_sb = B["T_sb"]
    cc_mask = B["cc_mask"]
    maskT = B["maskT"]
    ps4 = [B["ps_a"], B["ps_b"], B["ps_c"], B["ps_d"]]
    ps_ab = [B["ps_a"], B["ps_b"]]
    ps_y = B["ps_c"]
    ps_st = [B["ps_e"], B["ps_f"]]
    HP = 16

    for s in ("sqkv0", "sqkv1", "scc", "sqs0", "sqs1", "sql0", "sql1"):
        p.new_sem(s)

    wq4 = wqkv_d.rearrange("(kc p) n -> p kc n", p=128)
    wo4 = wo_d.rearrange("(kc p) n -> p kc n", p=128)

    ck_ev = {}
    ck_lnk = {}
    nld = [0]

    def load_wtile(src_ap, wait_pe=None):
        i = nld[0]
        nld[0] += 1
        sem = "sqkv%d" % (i % 2)
        if wait_pe is not None:
            p.wait("gpsimd", "s_pe", wait_pe)
        ck = p.inc("gpsimd", sem, 16,
                   lambda E, src_ap=src_ap, i=i: E.dma_start(out=wt[i % 2], in_=src_ap))
        return sem, ck, wt[i % 2]

    ck_mmgrp = {}
    for g in range(4):
        cs = [g * 4 + i for i in range(4)]
        for ci, c in enumerate(cs):
            xc = x_sb[:, c, :]
            p.emit("vector", lambda E, xc=xc: E.bn_stats(bn6[:, 0, :], xc[:, 0:512]))
            p.emit("vector", lambda E, xc=xc: E.bn_stats(bn6[:, 1, :], xc[:, 512:1024]))
            p.emit("vector", lambda E: E.drain())
            p.emit("vector", lambda E: E.bn_aggr(mv, bn6))
            p.emit("vector", lambda E: E.drain())
            p.inc("vector", "s_dve", 1,
                  lambda E: E.tensor_scalar(mv[:, 1:2], mv[:, 1:2], LN_EPS, None, ALU.add))
            p.wait("scalar", "s_dve")
            p.inc("scalar", "s_act", 1, lambda E: E.activation(sd, mv[:, 1:2], AF.Sqrt))
            p.wait("vector", "s_act")
            p.emit("vector", lambda E: E.reciprocal(rstd, sd))
            p.emit("vector", lambda E: E.drain())
            p.inc("vector", "s_dve", 1,
                  lambda E, xc=xc:
                  E.tensor_scalar(xln_t, xc, mv[:, 0:1], rstd, ALU.subtract, ALU.mult))
            for j in range(DCH):
                b = B["ps_t0"] if j % 2 == 0 else B["ps_t1"]
                p.wait("tensor", "s_dve")
                ck = p.inc("tensor", "s_pe", 1,
                           lambda E, j=j, b=b:
                           E.transpose(b, xln_t[:, j * 128:(j + 1) * 128], B["eye_bf"]))
                p.wait("vector", "s_pe", ck)
                p.inc("vector", "s_dve", 1,
                      lambda E, ci=ci, j=j, b=b:
                      E.tensor_copy(xr[:, ci, j, :], b))
        xr_ready = p.counts["s_dve"]
        p.wait("tensor", "s_dve", xr_ready)
        for cg in range(6):
            for k in range(DCH):
                sem, ck_w, wtile = load_wtile(
                    wq4[:, k, cg * 512:(cg + 1) * 512],
                    wait_pe=ck_mmgrp.get(nld[0] - 2))
                p.wait("tensor", sem, ck_w)
                for ci in range(4):
                    prev_use = (cs[ci], cg - 1) if cg >= 1 else (cs[ci] - 4, 5)
                    if k == 0 and prev_use in ck_ev:
                        p.wait("tensor", "s_dve", ck_ev[prev_use])
                    if ci < 3:
                        p.emit("tensor",
                               lambda E, ci=ci, k=k, wtile=wtile:
                               E.matmul(ps4[ci], xr[:, ci, k, :], wtile,
                                        start=(k == 0), stop=(k == DCH - 1)))
                    else:
                        ck_mmgrp[nld[0] - 1] = p.inc(
                            "tensor", "s_pe", 1,
                            lambda E, ci=ci, k=k, wtile=wtile:
                            E.matmul(ps4[ci], xr[:, ci, k, :], wtile,
                                     start=(k == 0), stop=(k == DCH - 1)))
            mm_done = p.counts["s_pe"]
            for ci, c in enumerate(cs):
                p.wait("vector", "s_pe", mm_done)
                if cg < 2:
                    if cg == 0 and c >= 4:
                        p.wait("vector", "sqs%d" % ((c // 4 - 1) % 2))
                    dst = q_ch[:, ci, cg * 512:(cg + 1) * 512]
                elif cg < 4:
                    dst = kn_tm[:, c, (cg - 2) * 512:(cg - 1) * 512]
                else:
                    dst = v_tm[:, c, (cg - 4) * 512:(cg - 3) * 512]
                ck_ev[(c, cg)] = p.inc("vector", "s_dve", 1,
                                       lambda E, dst=dst, ci=ci:
                                       E.tensor_copy(dst, ps4[ci]))
        for ci, c in enumerate(cs):
            p.wait("sync", "s_dve", ck_ev[(c, 1)])
            p.inc("sync", "sqs%d" % ((c // 4) % 2), 16,
                  lambda E, c=c, ci=ci: E.dma_start(out=q_dram[c], in_=q_ch[:, ci, :]))
            k3 = kn_tm[:, c, :].rearrange("p (h d) -> p h d", d=64)
            D = lambda: p.emit("vector", lambda E: E.drain())
            D()
            p.emit("vector", lambda E, k3=k3: E.tensor_reduce(
                st16.rearrange("p (h o) -> p h o", o=1), k3, AX.X, ALU.add))
            D()
            p.emit("vector", lambda E: E.tensor_scalar(st16, st16, 1.0 / 64, None, ALU.mult))
            D()
            p.emit("vector", lambda E, k3=k3: E.tensor_tensor(
                ks32.rearrange("p (h d) -> p h d", d=64), k3,
                st16.rearrange("p (h o) -> p h o", o=1).broadcast_to([128, 16, 64]),
                ALU.subtract))
            D()
            p.emit("vector", lambda E: E.tensor_tensor(sq32, ks32, ks32, ALU.mult))
            D()
            p.emit("vector", lambda E: E.tensor_reduce(
                st16b.rearrange("p (h o) -> p h o", o=1),
                sq32.rearrange("p (h d) -> p h d", d=64), AX.X, ALU.add))
            D()
            p.inc("vector", "s_dve", 1,
                  lambda E: E.tensor_scalar(st16b, st16b, 1.0 / 64, LN_EPS,
                                            ALU.mult, ALU.add))
            p.wait("scalar", "s_dve")
            p.inc("scalar", "s_act", 1, lambda E: E.activation(st16b, st16b, AF.Sqrt))
            p.wait("vector", "s_act")
            p.emit("vector", lambda E: E.reciprocal(st16b, st16b))
            D()
            ck_lnk[c] = p.inc("vector", "s_dve", 1,
                              lambda E, c=c: E.tensor_tensor(
                                  kn_tm[:, c, :].rearrange("p (h d) -> p h d", d=64),
                                  ks32.rearrange("p (h d) -> p h d", d=64),
                                  st16b.rearrange("p (h o) -> p h o", o=1)
                                  .broadcast_to([128, 16, 64]),
                                  ALU.mult))
    qkv_done = p.counts["s_dve"]

    p.wait("tensor", "s_dve", qkv_done)
    for h in range(HP):
        ph, fs = (h % 2) * 64, h // 2
        for c in range(TCH):
            fn = (lambda E, h=h, c=c, ph=ph, fs=fs:
                  E.matmul(ps_st[0][ph:ph + 64, fs, :],
                           kn_tm[:, c, h * 64:(h + 1) * 64],
                           v_tm[:, c, h * 64:(h + 1) * 64],
                           start=(c == 0), stop=(c == TCH - 1)))
            if h == HP - 1 and c == TCH - 1:
                ck_T = p.inc("tensor", "s_pe", 1, fn)
            else:
                p.emit("tensor", fn)
    p.wait("vector", "s_pe", ck_T)
    p.emit("vector", lambda E: E.tensor_copy(
        T_sb.rearrange("p a b -> p (a b)"), ps_st[0].rearrange("p a b -> p (a b)")))
    ck_Tsb = p.inc("vector", "s_dve", 1, lambda E: E.drain())

    t_loc = nc.dram_tensor("t_loc", [1024, 64], F32, kind="Internal").ap()
    t_all = nc.dram_tensor("t_all", [8192, 64], F32, kind="Internal").ap()
    p.wait("sync", "s_dve", ck_Tsb)
    ck_tdma = p.inc("sync", "s_ld", 16,
                    lambda E: E.dma_start(
                        out=t_loc.rearrange("(fs ph pe) fd -> (ph pe) fs fd", ph=2, pe=64),
                        in_=T_sb))
    if not skip_cc:
        p.wait("gpsimd", "s_ld", ck_tdma)
        p.inc("gpsimd", "scc", 1,
              lambda E: E.collective_compute(
                  "AllGather", ALU.bypass,
                  replica_groups=[[0, 1, 2, 3, 4, 5, 6, 7]],
                  ins=[t_loc.opt()],
                  outs=[t_all.opt()]))
        cc_done = p.counts["scc"]
    p.emit("vector", lambda E: E.memset(ST32.rearrange("p a b -> p (a b)"), 0.0))
    p.emit("vector", lambda E: E.drain())
    if not skip_cc:
        tr_sb = B["tr_sb"]
        for r in range(8):
            p.wait("sync", "scc", cc_done)
            if r >= 1:
                p.wait("sync", "s_dve")
            ck_r = p.inc("sync", "s_ld", 16,
                         lambda E, r=r: E.dma_start(
                             out=tr_sb,
                             in_=t_all[r * 1024:(r + 1) * 1024, :]
                             .rearrange("(fs ph pe) fd -> (ph pe) fs fd", ph=2, pe=64)))
            p.wait("vector", "s_ld", ck_r)
            p.inc("vector", "s_dve", 1,
                  lambda E, r=r: E.scalar_tensor_tensor(
                      ST32.rearrange("p a b -> p (a b)"),
                      tr_sb.rearrange("p a b -> p (a b)"),
                      cc_mask[:, r:r + 1],
                      ST32.rearrange("p a b -> p (a b)"), ALU.mult, ALU.add))
            p.emit("vector", lambda E: E.drain())
    p.emit("vector", lambda E: E.tensor_copy(
        STbf.rearrange("p a b -> p (a b)"), ST32.rearrange("p a b -> p (a b)")))
    st_ready = p.inc("vector", "s_dve", 1, lambda E: E.drain())

    ck_cast = {-1: st_ready}
    ck_stadd = {}
    prev_yev = {}
    ck_qld = {}
    ck_qtr = {}
    p.wait("sync", "sqs0")
    p.wait("sync", "sqs1")
    ck_qld[0] = p.inc("sync", "sql0", 16, lambda E: E.dma_start(out=q_ld[0], in_=q_dram[0]))
    for c in range(TCH):
        if c + 1 < TCH:
            if (c - 1) in ck_qtr:
                p.wait("sync", "s_pe", ck_qtr[c - 1])
            ck_qld[c + 1] = p.inc("sync", "sql%d" % ((c + 1) % 2), 16,
                                  lambda E, c=c: E.dma_start(out=q_ld[(c + 1) % 2],
                                                             in_=q_dram[c + 1]))
        p.wait("tensor", "sql%d" % (c % 2), ck_qld[c])
        last_ev = None
        last_qtr = None
        for (srcname, dstrot) in (("q", q_rot), ("kn", kn_rot)):
            for j in range(8):
                b = B["ps_t0"] if j % 2 == 0 else B["ps_t1"]
                p.wait("tensor", "s_dve")
                if srcname == "q":
                    ck = p.inc("tensor", "s_pe", 1,
                               lambda E, c=c, j=j, b=b:
                               E.transpose(b, q_ld[c % 2][:, j * 128:(j + 1) * 128],
                                           B["eye_bf"]))
                    last_qtr = ck
                else:
                    ck = p.inc("tensor", "s_pe", 1,
                               lambda E, c=c, j=j, b=b:
                               E.transpose(b, kn_tm[:, c, j * 128:(j + 1) * 128],
                                           B["eye_bf"]))
                p.wait("vector", "s_pe", ck)
                last_ev = p.inc("vector", "s_dve", 1,
                                lambda E, dstrot=dstrot, j=j, b=b:
                                E.tensor_copy(dstrot[:, j, :], b))
        ck_qtr[c] = last_qtr
        p.wait("tensor", "s_dve", last_ev)
        p.wait("tensor", "s_dve", ck_cast[c - 1])
        ck_scev = {}
        ck_ymm = {}
        for h in range(HP):
            ph, fs = (h % 2) * 64, h // 2
            bank = ps_ab[h % 2]
            if h >= 2:
                p.wait("tensor", "s_dve", ck_scev[h - 2])
            ck_sc = p.inc("tensor", "s_pe", 1,
                          lambda E, h=h, ph=ph, fs=fs, bank=bank:
                          E.matmul(bank[:, 0:128], kn_rot[ph:ph + 64, fs, :],
                                   q_rot[ph:ph + 64, fs, :], start=True, stop=True))
            p.wait("vector", "s_pe", ck_sc)
            ck_scev[h] = p.inc("vector", "s_dve", 1,
                               lambda E, h=h, bank=bank:
                               E.tensor_tensor(scm[h % 2], bank[:, 0:128], maskT,
                                               ALU.mult))
            p.wait("tensor", "s_dve", ck_scev[h])
            if h % 2 == 0:
                _pk = h // 2 - 1 if h >= 2 else 7
                if _pk in prev_yev:
                    p.wait("tensor", "s_dve", prev_yev[_pk])
            p.emit("tensor",
                   lambda E, h=h, c=c, ph=ph:
                   E.matmul(ps_y[ph:ph + 64, 0:128], v_tm[:, c, h * 64:(h + 1) * 64],
                            scm[h % 2], start=True, stop=False))
            ck_ymm[h] = p.inc("tensor", "s_pe", 1,
                              lambda E, h=h, ph=ph, fs=fs:
                              E.matmul(ps_y[ph:ph + 64, 0:128], STbf[ph:ph + 64, fs, :],
                                       q_rot[ph:ph + 64, fs, :], start=False, stop=True))
            if h % 2 == 1:
                p.wait("vector", "s_pe", ck_ymm[h])
                prev_yev[h // 2] = p.inc(
                    "vector", "s_dve", 1,
                    lambda E, c=c, j=h // 2:
                    E.tensor_copy(y_fm[:, j, c * 128:(c + 1) * 128], ps_y[:, 0:128]))
            if h == 0:
                if c == 0:
                    p.wait("tensor", "s_dve", ck_Tsb)
                elif (c - 2) in ck_stadd:
                    p.wait("tensor", "s_dve", ck_stadd[c - 2])
            fn_st = (lambda E, h=h, c=c, ph=ph, fs=fs:
                     E.matmul(ps_st[c % 2][ph:ph + 64, fs, :],
                              kn_tm[:, c, h * 64:(h + 1) * 64],
                              v_tm[:, c, h * 64:(h + 1) * 64], start=True, stop=True))
            if h == HP - 1:
                ck_allst = p.inc("tensor", "s_pe", 1, fn_st)
            else:
                p.emit("tensor", fn_st)

        p.wait("vector", "s_pe", ck_allst)
        ck_stadd[c] = p.inc("vector", "s_dve", 1,
                            lambda E, c=c: E.tensor_tensor(
                                ST32.rearrange("p a b -> p (a b)"),
                                ST32.rearrange("p a b -> p (a b)"),
                                ps_st[c % 2].rearrange("p a b -> p (a b)"), ALU.add))
        p.emit("vector", lambda E: E.drain())
        ck_cast[c] = p.inc("vector", "s_dve", 1,
                           lambda E: E.tensor_copy(STbf.rearrange("p a b -> p (a b)"),
                                                   ST32.rearrange("p a b -> p (a b)")))

    y_done = max(prev_yev.values())
    p.wait("tensor", "s_dve", y_done)
    ck_oev = {}
    ck_omm = {}
    for g in range(4):
        cs = [g * 4 + i for i in range(4)]
        for og in range(2):
            for k in range(DCH):
                sem, ck_w, wtile = load_wtile(
                    wo4[:, k, og * 512:(og + 1) * 512],
                    wait_pe=ck_mmgrp.get(nld[0] - 2))
                p.wait("tensor", sem, ck_w)
                for ci in range(4):
                    prev_use = (cs[ci], og - 1) if og >= 1 else (cs[ci] - 4, 1)
                    if k == 0 and prev_use in ck_oev:
                        p.wait("tensor", "s_dve", ck_oev[prev_use])
                    if ci < 3:
                        p.emit("tensor",
                               lambda E, ci=ci, k=k, wtile=wtile, cc=cs[ci]:
                               E.matmul(ps4[ci], y_fm[:, k, cc * 128:(cc + 1) * 128],
                                        wtile, start=(k == 0), stop=(k == DCH - 1)))
                    else:
                        ck_mmgrp[nld[0] - 1] = p.inc(
                            "tensor", "s_pe", 1,
                            lambda E, ci=ci, k=k, wtile=wtile, cc=cs[ci]:
                            E.matmul(ps4[ci], y_fm[:, k, cc * 128:(cc + 1) * 128],
                                     wtile, start=(k == 0), stop=(k == DCH - 1)))
            mm_done = p.counts["s_pe"]
            for ci, c in enumerate(cs):
                p.wait("vector", "s_pe", mm_done)
                ck_oev[(c, og)] = p.inc(
                    "vector", "s_dve", 1,
                    lambda E, c=c, og=og, ci=ci:
                    E.tensor_tensor(x_sb[:, c, og * 512:(og + 1) * 512],
                                    x_sb[:, c, og * 512:(og + 1) * 512],
                                    ps4[ci], ALU.add))


def emit_moe(p, x_sb, xln_fm, B, rwt_d, w1t_d, w2t_d, b1r_d, b2_d):
    nc = p.nc
    lg = B["lg"]
    top8 = B["top8"]
    m1, m2, s1, s2 = B["m1"], B["m2"], B["s1"], B["s2"]
    i1, i2 = B["i1"], B["i2"]
    route_tm = B["route_tm"]
    route_bf = B["route_bf"]
    route_fm = B["route_fm"]
    rwt_sb = B["rwt_sb"]
    b2_sb = B["b2_sb"]
    b1c = B["b1c"]
    h1g = B["h1g"]
    w2q = B["w2q"]
    w1s = [B["w1s0"], B["w1s1"]]
    ps_ab = [B["ps_a"], B["ps_b"]]
    ps_cd = [B["ps_c"], B["ps_d"]]
    ps_e = B["ps_e"]
    ps_e8 = ps_e.rearrange("p a b -> p (a b)")[:, 0:8] if hasattr(ps_e, "rearrange") else ps_e

    for s in ("sw1a", "sw1b", "sw2", "sb1", "smoe"):
        p.new_sem(s)

    w14 = w1t_d.rearrange("e (kc p) f -> e p kc f", p=128)
    w24 = w2t_d.rearrange("e (fc p) h -> e p fc h", p=128)

    p.inc("gpsimd", "smoe", 16,
          lambda E: E.dma_start(out=rwt_sb, in_=rwt_d.rearrange("(kc p) e -> p kc e", p=128)))
    p.inc("gpsimd", "smoe", 16, lambda E: E.dma_start(out=b2_sb, in_=b2_d))
    rw_ready = p.counts["smoe"]

    p.wait("tensor", "s_dve")
    p.wait("tensor", "smoe", rw_ready)
    ck_rt = {}
    for c in range(TCH):
        if c >= 1:
            p.wait("tensor", "s_dve", ck_rt[c - 1])
        for k in range(DCH - 1):
            p.emit("tensor",
                   lambda E, c=c, k=k:
                   E.matmul(ps_e8, xln_fm[:, k, c * 128:(c + 1) * 128],
                            rwt_sb[:, k, :], start=(k == 0), stop=False))
        ck_l = p.inc("tensor", "s_pe", 1,
                     lambda E, c=c:
                     E.matmul(ps_e8, xln_fm[:, DCH - 1, c * 128:(c + 1) * 128],
                              rwt_sb[:, DCH - 1, :], start=False, stop=True))
        p.wait("vector", "s_pe", ck_l)
        p.emit("vector", lambda E: E.tensor_copy(lg, ps_e8))
        ck_rt[c] = p.inc("vector", "s_dve", 1, lambda E: E.drain())
        p.emit("vector", lambda E: E.max(top8, lg))
        p.emit("vector", lambda E: E.drain())
        p.emit("vector", lambda E: E.tensor_copy(m1, top8[:, 0:1]))
        p.emit("vector", lambda E: E.tensor_copy(m2, top8[:, 1:2]))
        p.emit("vector", lambda E: E.drain())
        p.inc("vector", "s_dve", 1,
              lambda E: E.tensor_tensor(s1, m1, m2, ALU.subtract))
        p.wait("scalar", "s_dve")
        p.inc("scalar", "s_act", 1, lambda E: E.activation(s1, s1, AF.Sigmoid))
        p.wait("vector", "s_act")
        p.emit("vector", lambda E: E.tensor_scalar(s2, s1, -1.0, 1.0, ALU.mult, ALU.add))
        p.emit("vector", lambda E: E.drain())
        p.emit("vector", lambda E: E.tensor_scalar(i1, lg, m1, None, ALU.is_equal))
        p.emit("vector", lambda E: E.drain())
        p.emit("vector", lambda E: E.tensor_scalar(i2, lg, m2, None, ALU.is_equal))
        p.emit("vector", lambda E: E.drain())
        p.emit("vector", lambda E: E.tensor_scalar(i1, i1, s1, None, ALU.mult))
        p.emit("vector", lambda E: E.drain())
        p.emit("vector", lambda E: E.tensor_scalar(i2, i2, s2, None, ALU.mult))
        p.emit("vector", lambda E: E.drain())
        p.emit("vector", lambda E, c=c: E.tensor_tensor(route_tm[:, c, :], i1, i2, ALU.add))
        p.emit("vector", lambda E: E.drain())
        p.inc("vector", "s_dve", 1,
              lambda E, c=c: E.tensor_copy(route_bf, route_tm[:, c, :]))
        p.wait("tensor", "s_dve")
        ck_tr = p.inc("tensor", "s_pe", 1,
                      lambda E: E.transpose(B["ps_t0"][0:8, 0:128], route_bf, B["eye_bf"]))
        p.wait("vector", "s_pe", ck_tr)
        p.inc("vector", "s_dve", 1,
              lambda E, c=c: E.tensor_copy(route_fm[:, c * 128:(c + 1) * 128],
                                           B["ps_t0"][0:8, 0:128]))
    route_done = p.counts["s_dve"]

    p.wait("tensor", "s_dve", route_done)
    ck_b2 = {}
    for c in range(TCH):
        for hg in range(2):
            if (c - 1, hg) in ck_b2:
                p.wait("tensor", "s_dve", ck_b2[(c - 1, hg)])
            ck = p.inc("tensor", "s_pe", 1,
                       lambda E, c=c, hg=hg:
                       E.matmul(ps_cd[hg], route_fm[:, c * 128:(c + 1) * 128],
                                b2_sb[:, hg * 512:(hg + 1) * 512],
                                start=True, stop=True))
            p.wait("vector", "s_pe", ck)
            ck_b2[(c, hg)] = p.inc("vector", "s_dve", 1,
                                   lambda E, c=c, hg=hg:
                                   E.tensor_tensor(x_sb[:, c, hg * 512:(hg + 1) * 512],
                                                   x_sb[:, c, hg * 512:(hg + 1) * 512],
                                                   ps_cd[hg], ALU.add))

    nw1 = [0]
    ck_w1mm = {}

    def load_w1(e, fq, fc):
        i = nw1[0]
        nw1[0] += 1
        sem = "sw1a" if i % 2 == 0 else "sw1b"
        if i >= 2 and (i - 2) in ck_w1mm:
            p.wait("gpsimd", "s_pe", ck_w1mm[i - 2])
        f0 = fq * 1024 + fc * 128
        ck = p.inc("gpsimd", sem, 16,
                   lambda E, e=e, f0=f0, i=i:
                   E.dma_start(out=w1s[i % 2], in_=w14[e][:, :, f0:f0 + 128]))
        return sem, ck, w1s[i % 2]

    ck_b1 = {}
    ck_w2ph = {}
    prev_h1_last = [0]
    ck_oev = dict(ck_b2)
    for e in range(8):
        p.wait("gpsimd", "s_act")
        ck_b1[e] = p.inc("gpsimd", "sb1", 16,
                         lambda E, e=e: E.dma_start(out=b1c[:, e % 2, :], in_=b1r_d[e]))
        for fq in range(4):
            if (e, fq - 1) in ck_w2ph:
                p.wait("gpsimd", "s_pe", ck_w2ph[(e, fq - 1)])
            elif (e - 1, 3) in ck_w2ph:
                p.wait("gpsimd", "s_pe", ck_w2ph[(e - 1, 3)])
            ck_w2 = p.inc("gpsimd", "sw2", 16,
                          lambda E, e=e, fq=fq:
                          E.dma_start(out=w2q, in_=w24[e][:, fq * 8:(fq + 1) * 8, :]))
            ck_h1ev = {}
            for fc in range(8):
                sem, ck_w, wtile = load_w1(e, fq, fc)
                p.wait("tensor", sem, ck_w)
                for tt in range(4):
                    gidx = fc * 4 + tt
                    bank = ps_ab[gidx % 2]
                    if (gidx - 2) in ck_h1ev:
                        p.wait("tensor", "s_act", ck_h1ev[gidx - 2])
                    elif (e, fq) != (0, 0) and gidx < 2:
                        p.wait("tensor", "s_act", prev_h1_last[0])
                    for k in range(DCH - 1):
                        p.emit("tensor",
                               lambda E, k=k, tt=tt, wtile=wtile, bank=bank:
                               E.matmul(bank, wtile[:, k, :],
                                        xln_fm[:, k, tt * 512:(tt + 1) * 512],
                                        start=(k == 0), stop=False))
                    ck_h1 = p.inc("tensor", "s_pe", 1,
                                  lambda E, tt=tt, wtile=wtile, bank=bank:
                                  E.matmul(bank, wtile[:, DCH - 1, :],
                                           xln_fm[:, DCH - 1, tt * 512:(tt + 1) * 512],
                                           start=False, stop=True))
                    ck_w1mm[nw1[0] - 1] = ck_h1
                    p.wait("scalar", "s_pe", ck_h1)
                    if (e, fq) == (0, 0) and fc == 0 and tt == 0:
                        p.wait("scalar", "sb1", ck_b1[e])
                    elif tt == 0 and fc == 0:
                        p.wait("scalar", "sb1", ck_b1[e])
                    ck_h1ev[gidx] = p.inc(
                        "scalar", "s_act", 1,
                        lambda E, e=e, fq=fq, fc=fc, tt=tt, bank=bank:
                        E.activation(h1g[:, fc, tt * 512:(tt + 1) * 512], bank,
                                     AF.Gelu, bias=b1c[:, e % 2, fq * 8 + fc:fq * 8 + fc + 1]))
            h1_done = p.counts["s_act"]
            prev_h1_last[0] = h1_done
            p.wait("tensor", "s_act", h1_done)
            p.wait("tensor", "sw2", ck_w2)
            last_mm = None
            for c in range(TCH):
                for hg in range(2):
                    bank = ps_cd[(c * 2 + hg) % 2]
                    prev = (c - 1, hg) if c >= 1 else (TCH - 1, hg)
                    if prev in ck_oev:
                        p.wait("tensor", "s_dve", ck_oev[prev])
                    for fc in range(7):
                        p.emit("tensor",
                               lambda E, c=c, hg=hg, fc=fc, bank=bank:
                               E.matmul(bank, h1g[:, fc, c * 128:(c + 1) * 128],
                                        w2q[:, fc, hg * 512:(hg + 1) * 512],
                                        start=(fc == 0), stop=False))
                    last_mm = p.inc("tensor", "s_pe", 1,
                                    lambda E, c=c, hg=hg, bank=bank:
                                    E.matmul(bank, h1g[:, 7, c * 128:(c + 1) * 128],
                                             w2q[:, 7, hg * 512:(hg + 1) * 512],
                                             start=False, stop=True))
                    p.wait("vector", "s_pe", last_mm)
                    ck_oev[(c, hg)] = p.inc(
                        "vector", "s_dve", 1,
                        lambda E, e=e, c=c, hg=hg, bank=bank:
                        E.scalar_tensor_tensor(x_sb[:, c, hg * 512:(hg + 1) * 512],
                                               bank, route_tm[:, c, e:e + 1],
                                               x_sb[:, c, hg * 512:(hg + 1) * 512],
                                               ALU.mult, ALU.add))
            ck_w2ph[(e, fq)] = last_mm


U8 = mybir.dt.uint8
ARENA_BYTES = 140 * 1024


def _carve_factory(arena):
    state = {"off": 0}

    def carve(shape, dtype, off=None):
        esz = {F32: 4, BF16: 2}[dtype]
        n = 1
        for s in shape[1:]:
            n *= s
        nbytes = n * esz
        if off is None:
            off = state["off"]
        state["off"] = off + ((nbytes + 63) // 64) * 64
        v = arena[:, off:off + nbytes].bitcast(dtype)
        if len(shape) == 2:
            return v
        if len(shape) == 3:
            return v.rearrange("p (a b) -> p a b", a=shape[1])
        if len(shape) == 4:
            return v.rearrange("p (a b c) -> p a b c", a=shape[1], b=shape[2])
        raise ValueError(shape)

    def reset():
        state["off"] = 0
    return carve, reset


def build_fused(skip_cc=False):
    nc = bass.Bass("TRN2", target_bir_lowering=False, debug=False)
    d = lambda n, s: nc.dram_tensor(n, s, F32, kind="ExternalInput").ap()
    x_d = d("x", [2048, 1024])
    kt_d = d("kt", [1024, 4096])
    v_d = d("v", [4096, 1024])
    wg_d = d("wg", [2816, 1024])
    wu_d = d("wu", [2816, 1024])
    wd_d = d("wd", [1024, 2816])
    wqkv_d = d("wqkv", [1024, 3072])
    wo_d = d("wo", [1024, 1024])
    rwt_d = d("rwt", [1024, 8])
    w1t_d = d("w1t", [8, 1024, 4096])
    w2t_d = d("w2t", [8, 4096, 1024])
    b1r_d = d("b1r", [8, 128, 32])
    b2_d = d("b2", [8, 1024])
    eye_d = d("eye", [128, 128])
    ones_d = d("ones", [128, 1])
    maskt_d = d("maskt", [128, 128])
    ccm_d = d("ccm", [128, 8])
    xo_d = nc.dram_tensor("xo", [2048, 1024], F32, kind="ExternalOutput").ap()

    a = lambda n, s, dt: nc.alloc_sbuf_tensor(n, s, dt).ap()
    x_sb = a("x_sb", [128, 16, 1024], F32)
    com = {
        "bn6": a("bn6", [128, 2, 6], F32),
        "mv": a("mv", [128, 2], F32),
        "sd": a("sd", [128, 1], F32),
        "rstd": a("rstd", [128, 1], F32),
        "eye_bf": a("eye_bf", [128, 128], BF16),
        "xln_t": a("xln_t", [128, 1024], BF16),
        "ones_bf": a("ones_bf", [128, 1], BF16),
        "ones_f32": a("ones_f32", [128, 1], F32),
        "cc_mask": a("cc_mask", [128, 8], F32),
        "maskT": a("maskT", [128, 128], BF16),
        "ps_t0": nc.alloc_psum_tensor("ps_t0", [128, 128], BF16).ap(),
        "ps_t1": nc.alloc_psum_tensor("ps_t1", [128, 128], BF16).ap(),
        "ps_a": nc.alloc_psum_tensor("ps_a", [128, 512], F32).ap(),
        "ps_b": nc.alloc_psum_tensor("ps_b", [128, 512], F32).ap(),
        "ps_c": nc.alloc_psum_tensor("ps_c", [128, 512], F32).ap(),
        "ps_d": nc.alloc_psum_tensor("ps_d", [128, 512], F32).ap(),
        "ps_e": nc.alloc_psum_tensor("ps_e", [128, 8, 64], F32).ap(),
        "ps_f": nc.alloc_psum_tensor("ps_f", [128, 8, 64], F32).ap(),
    }
    arena = nc.alloc_sbuf_tensor("arena", [128, ARENA_BYTES], U8).ap()
    carve, reset = _carve_factory(arena)

    p = Prog(nc)
    p.inc("sync", "s_ld", 16,
          lambda E: E.dma_start(out=x_sb, in_=x_d.rearrange("(c p) d -> p c d", p=128)))
    p.inc("gpsimd", "s_ldg", 16, lambda E: E.dma_start(out=com["eye_bf"], in_=eye_d))
    p.inc("gpsimd", "s_ldg", 16, lambda E: E.dma_start(out=com["ones_bf"], in_=ones_d))
    p.inc("sync", "s_ld", 16, lambda E: E.dma_start(out=com["ones_f32"], in_=ones_d))
    p.inc("gpsimd", "s_ldg", 16, lambda E: E.dma_start(out=com["maskT"], in_=maskt_d))
    p.inc("sync", "s_ld", 16, lambda E: E.dma_start(out=com["cc_mask"], in_=ccm_d))
    p.barrier()

    wgt_q = nc.dram_tensor("wgt_q", [8, 128, 2816], BF16, kind="Internal").ap()
    wut_q = nc.dram_tensor("wut_q", [8, 128, 2816], BF16, kind="Internal").ap()
    p.new_sem("swq")
    p.new_sem("swst0")
    p.new_sem("swst1")
    reset()
    B = dict(com)
    B.update({
        "ps_row": com["ps_e"].rearrange("p a b -> p (a b)")[0:1, 0:512],
        "ps_tcol": com["ps_f"].rearrange("p a b -> p (a b)")[:, 0:4],
    })
    xln_fm = carve([128, 8, 512], BF16)
    B["g_fm"] = carve([128, 32, 512], BF16)
    B["v_sb"] = carve([128, 32, 1024], BF16)
    B["kt0"] = carve([128, 8, 128], BF16)
    B["kt1"] = carve([128, 8, 128], BF16)
    B["g20"] = carve([128, 512], BF16)
    B["g21"] = carve([128, 512], BF16)
    B["ssrow"] = carve([128, 512], F32)[0:1, :]
    B["tmp4"] = carve([128, 4], F32)
    B["r64"] = carve([128, 16], F32)
    Qw = {
        "xn": carve([128, 1024], F32),
        "ax": carve([128, 1024], F32),
        "qa": carve([128, 1024], BF16),
        "qb": carve([128, 1024], BF16),
        "sgn": carve([128, 1024], BF16),
        "sc": carve([128, 64], F32),
        "rs": carve([128, 64], F32),
        "hs": carve([128, 64], F32),
        "dst_tm": carve([128, 1024], BF16),
    }
    B["Qw"] = Qw
    B["wqc"] = carve([128, 8, 128], BF16)
    wstage2 = carve([128, 2, 1024], F32)
    wg4 = wg_d.rearrange("(r p) d -> p r d", p=128)
    wu4 = wu_d.rearrange("(r p) d -> p r d", p=128)

    extra = []
    _wq_state = {"i": 0}

    _wq_cks = {}

    def _mk_tile(src4, dst_q, r):
        def emit_tile():
            i = _wq_state["i"]
            _wq_state["i"] += 1
            sem = "swst%d" % (i % 2)
            if i >= 2:
                p.wait("gpsimd", "s_dve", _wq_cks[i - 2])
            ck_st = p.inc("gpsimd", sem, 16,
                          lambda E, src4=src4, r=r, i=i:
                          E.dma_start(out=wstage2[:, i % 2, :], in_=src4[:, r, :]))
            emit_wq_tile(p, B, None,
                         dst_q[:, :, r * 128:(r + 1) * 128]
                         .rearrange("kc p i -> p kc i"),
                         wstage2[:, i % 2, :], sem, ck_st)
            _wq_cks[i] = p.counts["s_dve"]
        return emit_tile

    for r in range(22):
        extra.append(_mk_tile(wg4, wgt_q, r))
    for r in range(22):
        extra.append(_mk_tile(wu4, wut_q, r))

    emit_l0_attn(p, x_sb, xln_fm, B, kt_d, v_d, extra_work=extra)
    p.barrier()

    reset()
    B = dict(com)

    def mkQ():
        return {
            "xn": carve([128, 1024], F32),
            "ax": carve([128, 1024], F32),
            "qa": carve([128, 1024], BF16),
            "qb": carve([128, 1024], BF16),
            "sgn": carve([128, 1024], BF16),
            "sc": carve([128, 64], F32),
            "rs": carve([128, 64], F32),
            "hs": carve([128, 64], F32),
            "x32": carve([128, 1024], F32),
            "dst_tm": carve([128, 1024], BF16),
        }
    B["Qv"] = mkQ()
    B["Qg"] = mkQ()
    B["xq_fm"] = carve([128, 8, 2048], BF16)
    B["wd_raw"] = carve([128, 8, 512], F32)
    B["wgT"] = carve([128, 8, 512], BF16)
    B["wuT"] = carve([128, 8, 512], BF16)
    B["wdT"] = carve([128, 4, 1024], BF16)
    B["sg"] = carve([128, 2, 512], F32)
    B["h32"] = carve([128, 2, 512], F32)
    B["hq_fm"] = carve([128, 4, 128], BF16)
    emit_l0_ffn(p, x_sb, B, wgt_q, wut_q, wd_d)
    p.barrier()

    reset()
    B = dict(com)
    B["Q"] = {"xn": carve([128, 1024], F32)}
    B["sq_bf"] = carve([128, 1024], BF16)
    B["xr"] = carve([128, 4, 8, 128], BF16)
    B["q_ch"] = carve([128, 4, 1024], BF16)
    B["q_ld0"] = carve([128, 1024], BF16)
    B["q_ld1"] = carve([128, 1024], BF16)
    B["kn_tm"] = carve([128, 16, 1024], BF16)
    B["v_tm"] = carve([128, 16, 1024], BF16)
    B["y_fm"] = carve([128, 8, 2048], BF16)
    B["wt0"] = carve([128, 512], BF16)
    B["wt1"] = carve([128, 512], BF16)
    B["st16"] = carve([128, 16], F32)
    B["st16b"] = carve([128, 16], F32)
    B["q_rot"] = carve([128, 8, 128], BF16)
    B["kn_rot"] = carve([128, 8, 128], BF16)
    B["scm0"] = carve([128, 128], BF16)
    B["scm1"] = carve([128, 128], BF16)
    B["ST32"] = carve([128, 8, 64], F32)
    B["STbf"] = carve([128, 8, 64], BF16)
    B["T_sb"] = carve([128, 8, 64], F32)
    B["tr_sb"] = B["T_sb"]
    emit_l1_attn(p, x_sb, B, wqkv_d, wo_d, skip_cc=skip_cc)
    p.barrier()

    reset()
    B = dict(com)
    xln_fm = carve([128, 8, 2048], BF16)
    B["h1g"] = carve([128, 8, 2048], BF16)
    B["w2q"] = carve([128, 8, 1024], BF16)
    B["route_tm"] = carve([128, 16, 8], F32)
    B["route_bf"] = carve([128, 8], BF16)
    B["route_fm"] = carve([128, 2048], BF16)[0:8, :]
    B["rwt_sb"] = carve([128, 8, 8], BF16)
    B["b2_sb"] = carve([128, 1024], BF16)[0:8, :]
    B["b1c"] = carve([128, 2, 32], F32)
    B["w1s0"] = carve([128, 8, 128], BF16)
    B["w1s1"] = carve([128, 8, 128], BF16)
    B["lg"] = carve([128, 8], F32)
    B["top8"] = carve([128, 8], F32)
    B["m1"] = carve([128, 1], F32)
    B["m2"] = carve([128, 1], F32)
    B["s1"] = carve([128, 1], F32)
    B["s2"] = carve([128, 1], F32)
    B["i1"] = carve([128, 8], F32)
    B["i2"] = carve([128, 8], F32)
    emit_ln(p, x_sb, xln_fm, B)
    emit_moe(p, x_sb, xln_fm, B, rwt_d, w1t_d, w2t_d, b1r_d, b2_d)

    p.barrier(("sync",))
    p.inc("sync", "s_out", 16,
          lambda E: E.dma_start(out=xo_d.rearrange("(c p) d -> p c d", p=128), in_=x_sb))
    p.wait("sync", "s_out")
    p.build()
    return nc


def _make_ccm(core):
    m = np.zeros((128, 8), np.float32)
    b, qq = core // 4, core % 4
    for j in range(8):
        if j // 4 == b and j % 4 < qq:
            m[:, j] = 1.0
    return m


def kernel(L0_attn_inputs, L0_attn_key_param_tokens, L0_attn_value_param_tokens,
           L0_ffn_w_gate, L0_ffn_w_up, L0_ffn_w_down,
           L1_attn_W_qkv, L1_attn_W_o,
           L1_ffn_router_w, L1_ffn_w1, L1_ffn_b1, L1_ffn_w2, L1_ffn_b2):
    g = lambda t: np.ascontiguousarray(np.asarray(t, np.float32))
    x = g(L0_attn_inputs)
    ins = {
        "kt": np.ascontiguousarray(g(L0_attn_key_param_tokens).T),
        "v": g(L0_attn_value_param_tokens),
        "wg": g(L0_ffn_w_gate),
        "wu": g(L0_ffn_w_up),
        "wd": g(L0_ffn_w_down),
        "wqkv": g(L1_attn_W_qkv),
        "wo": g(L1_attn_W_o),
        "rwt": np.ascontiguousarray(g(L1_ffn_router_w).T),
        "w1t": np.ascontiguousarray(np.transpose(g(L1_ffn_w1), (0, 2, 1))),
        "w2t": np.ascontiguousarray(np.transpose(g(L1_ffn_w2), (0, 2, 1))),
        "b1r": np.ascontiguousarray(g(L1_ffn_b1).reshape(8, 32, 128).transpose(0, 2, 1)),
        "b2": g(L1_ffn_b2),
        "eye": np.eye(128, dtype=np.float32),
        "ones": np.ones((128, 1), np.float32),
        "maskt": np.triu(np.ones((128, 128), np.float32)),
    }
    nc = build_fused(skip_cc=False)
    in_maps = []
    for c in range(8):
        b, q = c // 4, c % 4
        m = dict(ins)
        m["x"] = np.ascontiguousarray(x[b, q * 2048:(q + 1) * 2048, :])
        m["ccm"] = _make_ccm(c)
        in_maps.append(m)
    res = run_bass_kernel_spmd(nc, in_maps, core_ids=list(range(8)))
    out = np.zeros((2, 8192, 1024), np.float32)
    for c in range(8):
        b, q = c // 4, c % 4
        out[b, q * 2048:(q + 1) * 2048, :] = res.results[c]["xo"]
    return out


# version 06 (final)
# speedup vs baseline: 2.1338x; 2.1338x over previous
import numpy as np
import concourse.bass as bass
import concourse.mybir as mybir
from concourse.bass_utils import run_bass_kernel_spmd

F32 = mybir.dt.float32
BF16 = mybir.dt.bfloat16
AF = mybir.ActivationFunctionType
ALU = mybir.AluOpType
AX = mybir.AxisListType

T = 2048
D = 1024
TCH = 16
DCH = 8
LN_EPS = 1e-5

ENGINES = ("sync", "tensor", "vector", "scalar", "gpsimd")
SEMS = ("s_ld", "s_ldg", "s_pe", "s_act", "s_dve", "s_gp", "s_out")


class Prog:
    def __init__(self, nc):
        self.nc = nc
        self.q = {e: [] for e in ENGINES}
        self.sems = {}
        self.counts = {}
        self._sem_guards = []
        for s in SEMS:
            self._mk_sem(s)

    def new_sem(self, name):
        if name not in self.sems:
            self._mk_sem(name)
        return name

    def _mk_sem(self, name):
        guard = self.nc.semaphore(name)
        h = guard.__enter__()
        self._sem_guards.append(guard)
        self.sems[name] = h
        self.counts[name] = 0

    def emit(self, eng, fn):
        self.q[eng].append(fn)

    def wait(self, eng, sem, val=None):
        v = self.counts[sem] if val is None else val
        h = self.sems[sem]
        if v > 0:
            self.q[eng].append(lambda E, h=h, v=v: E.wait_ge(h, v))

    def inc(self, eng, sem, n, fn):
        h = self.sems[sem]
        self.counts[sem] += n
        self.q[eng].append(lambda E, fn=fn, h=h, n=n: fn(E).then_inc(h, n))
        return self.counts[sem]

    def barrier(self, engs=ENGINES):
        for e in engs:
            for s in SEMS:
                self.wait(e, s)

    def build(self):
        with self.nc.Block() as block:
            for e in ENGINES:
                items = self.q[e]
                if not items:
                    continue

                def body(E, items=items):
                    for fn in items:
                        fn(E)
                getattr(block, e)(body)
        for g in reversed(self._sem_guards):
            g.__exit__(None, None, None)


def emit_ln(p, x_sb, xln_fm, B, fp32_rows=None):
    bn6, mv, sd, rstd = B["bn6"], B["mv"], B["sd"], B["rstd"]
    xln_t, eye_bf = B["xln_t"], B["eye_bf"]

    for c in range(TCH):
        xc = x_sb[:, c, :]
        p.emit("vector", lambda E, xc=xc: E.bn_stats(bn6[:, 0, :], xc[:, 0:512]))
        p.emit("vector", lambda E, xc=xc: E.bn_stats(bn6[:, 1, :], xc[:, 512:1024]))
        p.emit("vector", lambda E: E.drain())
        p.emit("vector", lambda E: E.bn_aggr(mv, bn6))
        p.emit("vector", lambda E: E.drain())
        p.inc("vector", "s_dve", 1,
              lambda E: E.tensor_scalar(mv[:, 1:2], mv[:, 1:2], LN_EPS, None, ALU.add))
        p.wait("scalar", "s_dve")
        p.inc("scalar", "s_act", 1, lambda E: E.activation(sd, mv[:, 1:2], AF.Sqrt))
        p.wait("vector", "s_act")
        p.emit("vector", lambda E: E.reciprocal(rstd, sd))
        p.emit("vector", lambda E: E.drain())
        p.inc("vector", "s_dve", 1,
              lambda E, xc=xc:
              E.tensor_scalar(xln_t, xc, mv[:, 0:1], rstd, ALU.subtract, ALU.mult))
        if fp32_rows is not None:
            p.inc("vector", "s_dve", 1,
                  lambda E, xc=xc, o=fp32_rows[:, c, :]:
                  E.tensor_scalar(o, xc, mv[:, 0:1], rstd, ALU.subtract, ALU.mult))
        tr_evac = {}
        for j in range(DCH):
            if j < 2:
                p.wait("tensor", "s_dve")
            else:
                p.wait("tensor", "s_dve", tr_evac[j - 2])
            ck = p.inc("tensor", "s_pe", 1,
                       lambda E, j=j, b=B["ps_t0" if j % 2 == 0 else "ps_t1"]:
                       E.transpose(b, xln_t[:, j * 128:(j + 1) * 128], eye_bf))
            p.wait("vector", "s_pe", ck)
            tr_evac[j] = p.inc("vector", "s_dve", 1,
                               lambda E, c=c, j=j, b=B["ps_t0" if j % 2 == 0 else "ps_t1"]:
                               E.tensor_copy(xln_fm[:, j, c * 128:(c + 1) * 128], b))


def emit_l0_attn(p, x_sb, xln_fm, B, kt_d, v_d, stage=3, extra_work=None):
    SCH = 32
    bn6, mv, sd, rstd = B["bn6"], B["mv"], B["sd"], B["rstd"]
    xln_t = B["xln_t"]

    def emit_ln_quarter(q):
        dst = xln_fm
        p.wait("tensor", "s_pe")
        for ci in range(4):
            c = q * 4 + ci
            xc = x_sb[:, c, :]
            p.emit("vector", lambda E, xc=xc: E.bn_stats(bn6[:, 0, :], xc[:, 0:512]))
            p.emit("vector", lambda E, xc=xc: E.bn_stats(bn6[:, 1, :], xc[:, 512:1024]))
            p.emit("vector", lambda E: E.drain())
            p.emit("vector", lambda E: E.bn_aggr(mv, bn6))
            p.emit("vector", lambda E: E.drain())
            p.inc("vector", "s_dve", 1,
                  lambda E: E.tensor_scalar(mv[:, 1:2], mv[:, 1:2], LN_EPS, None, ALU.add))
            p.wait("scalar", "s_dve")
            p.inc("scalar", "s_act", 1, lambda E: E.activation(sd, mv[:, 1:2], AF.Sqrt))
            p.wait("vector", "s_act")
            p.emit("vector", lambda E: E.reciprocal(rstd, sd))
            p.emit("vector", lambda E: E.drain())
            p.inc("vector", "s_dve", 1,
                  lambda E, xc=xc:
                  E.tensor_scalar(xln_t, xc, mv[:, 0:1], rstd, ALU.subtract, ALU.mult))
            emit_tm_to_fm(p, xln_t,
                          [(j, dst[:, j, ci * 128:(ci + 1) * 128]) for j in range(DCH)],
                          B)
        return p.counts["s_dve"]
    kt_sb = [B["kt0"], B["kt1"]]
    g_fm = B["g_fm"]
    g2 = [B["g20"], B["g21"]]
    v_sb = B["v_sb"]
    ps_ab = [B["ps_a"], B["ps_b"]]
    ps_cd = [B["ps_c"], B["ps_d"]]
    ps_row = B["ps_row"]
    ps_tcol = B["ps_tcol"]
    ones_bf = B["ones_bf"]
    ones_f32 = B["ones_f32"]
    ssrow = B["ssrow"]
    tmp4 = B["tmp4"]
    r64 = B["r64"]

    kt4 = kt_d.rearrange("(c p) s -> p c s", p=128)
    v4 = v_d.rearrange("(c p) d -> p c d", p=128)
    p.new_sem("kt_p0")
    p.new_sem("kt_p1")
    p.new_sem("sv")

    prev_last_mm = None
    prev_gelu_last = None
    prev_ss_copy = None
    prev_bev = {}
    first_ss_bank_touch = [True]
    n_extra = len(extra_work) if extra_work else 0
    ln_ck = {}
    ln_ck[0] = emit_ln_quarter(0)

    import os as _os
    for i in range(4 if _os.environ.get("NOV") != "1" else 0):
        p.inc("gpsimd", "sv", 16,
              lambda E, i=i:
              E.dma_start(out=v_sb[:, i * 8:(i + 1) * 8, :],
                          in_=v4[:, i * 8:(i + 1) * 8, :]))
    sv_total = p.counts["sv"]

    for q in range(4):
        xq_cur = xln_fm
        if extra_work:
            k0 = (n_extra * q) // 4
            k1 = (n_extra * (q + 1)) // 4
            for w in extra_work[k0:k1]:
                w()
        p.wait("tensor", "s_dve", ln_ck[q])
        ck_mm = {}
        ck_ss = {}
        ck_gelu = {}
        ck_sq = {}

        def emit_ss(s, q=q, ck_sq=ck_sq, ck_ss=ck_ss):
            p.wait("tensor", "s_dve", ck_sq[s])
            if s == 0 and prev_ss_copy is not None:
                p.wait("tensor", "s_dve", prev_ss_copy)
            ck_ss[s] = p.inc("tensor", "s_pe", 1,
                             lambda E, s=s:
                             E.matmul(ps_row, ones_bf, g2[s % 2],
                                      start=(s == 0), stop=(s == SCH - 1)))

        for s in range(SCH):
            buf = s % 2
            if (s - 2) in ck_mm:
                p.wait("gpsimd", "s_pe", ck_mm[s - 2])
            elif prev_last_mm is not None:
                p.wait("gpsimd", "s_pe", prev_last_mm)
            ktsem = "kt_p%d" % buf
            ck_kt = p.inc("gpsimd", ktsem, 16,
                          lambda E, s=s, buf=buf:
                          E.dma_start(out=kt_sb[buf], in_=kt4[:, :, s * 128:(s + 1) * 128]))
            p.wait("tensor", ktsem, ck_kt)
            if _os.environ.get("NOMM") == "1":
                ck_mm[s] = p.counts["s_pe"]
                ck_gelu[s - 0] = ck_gelu.get(s)
                ck_sq[s] = p.counts["s_act"]
                continue
            if (s - 2) in ck_gelu:
                p.wait("tensor", "s_act", ck_gelu[s - 2])
            elif prev_gelu_last is not None:
                p.wait("tensor", "s_act", prev_gelu_last)
            if s < 2 and (6 + s) in prev_bev:
                p.wait("tensor", "s_dve", prev_bev[6 + s])
            for k in range(DCH - 1):
                p.emit("tensor",
                       lambda E, s=s, k=k, buf=buf, xq_cur=xq_cur:
                       E.matmul(ps_ab[s % 2], kt_sb[buf][:, k, :],
                                xq_cur[:, k, :],
                                start=(k == 0), stop=False))
            ck_mm[s] = p.inc("tensor", "s_pe", 1,
                             lambda E, s=s, buf=buf, xq_cur=xq_cur:
                             E.matmul(ps_ab[s % 2], kt_sb[buf][:, DCH - 1, :],
                                      xq_cur[:, DCH - 1, :],
                                      start=False, stop=True))
            if s >= 1 and stage >= 2:
                emit_ss(s - 1)
            if _os.environ.get("NOGELU") == "1":
                ck_sq[s] = p.counts["s_act"]
                continue
            p.wait("scalar", "s_pe", ck_mm[s])
            ck_gelu[s] = p.inc(
                "scalar", "s_act", 1,
                lambda E, s=s: E.activation(g_fm[:, s, :], ps_ab[s % 2], AF.Gelu))
            if stage < 2:
                ck_sq[s] = p.counts["s_dve"]
                continue
            p.wait("vector", "s_act", ck_gelu[s])
            if (s - 2) in ck_ss:
                p.wait("vector", "s_pe", ck_ss[s - 2])
            ck_sq[s] = p.inc("vector", "s_dve", 1,
                             lambda E, s=s:
                             E.tensor_tensor(g2[s % 2], g_fm[:, s, :], g_fm[:, s, :],
                                             ALU.mult))
        if stage >= 2:
            emit_ss(SCH - 1)
        prev_last_mm = ck_mm[SCH - 1]
        if ck_gelu:
            prev_gelu_last = ck_gelu[SCH - 1]
        if q + 1 < 4:
            ln_ck[q + 1] = emit_ln_quarter(q + 1)
        if stage < 2:
            continue

        p.wait("vector", "s_pe", ck_ss[SCH - 1])
        p.emit("vector", lambda E: E.tensor_copy(ssrow, ps_row))
        prev_ss_copy = p.inc("vector", "s_dve", 1, lambda E: E.drain())
        p.wait("tensor", "s_dve", prev_ss_copy)
        for j in range(3):
            p.emit("tensor",
                   lambda E, j=j:
                   E.matmul(ps_tcol[:, j:j + 1], ssrow[0:1, j * 128:(j + 1) * 128],
                            ones_f32[0:1, 0:1], start=(j == 0), stop=False))
        ck_tc = p.inc("tensor", "s_pe", 1,
                      lambda E:
                      E.matmul(ps_tcol[:, 3:4], ssrow[0:1, 384:512],
                               ones_f32[0:1, 0:1], start=False, stop=True))
        p.wait("vector", "s_pe", ck_tc)
        p.emit("vector", lambda E: E.tensor_copy(tmp4, ps_tcol))
        p.inc("vector", "s_dve", 1, lambda E: E.drain())
        p.wait("scalar", "s_dve")
        p.inc("scalar", "s_act", 1,
              lambda E: E.activation(tmp4, tmp4, AF.Sqrt, scale=1.0 / 4096.0))
        p.wait("vector", "s_act")
        p.emit("vector", lambda E, q=q: E.reciprocal(r64[:, q * 4:(q + 1) * 4], tmp4))
        r64_done = p.inc("vector", "s_dve", 1, lambda E: E.drain())

        ck_bev = {}
        if stage < 3:
            continue
        for j in range(4):
            c = q * 4 + j
            pair = [ps_ab, ps_cd][j % 2]
            if j >= 2 and (j - 2) in ck_bev:
                p.wait("tensor", "s_dve", ck_bev[j - 2])
            elif j == 0:
                p.wait("tensor", "s_act", ck_gelu[SCH - 1])
            elif j == 1 and (3 in prev_bev):
                p.wait("tensor", "s_dve", prev_bev[3])
            if q == 0 and j == 0:
                p.wait("tensor", "sv", sv_total)
            for s in range(SCH):
                for dv in range(2):
                    if s == SCH - 1 and dv == 1:
                        continue
                    p.emit("tensor",
                           lambda E, s=s, dv=dv, j=j, pair=pair:
                           E.matmul(pair[dv], g_fm[:, s, j * 128:(j + 1) * 128],
                                    v_sb[:, s, dv * 512:(dv + 1) * 512],
                                    start=(s == 0), stop=(s == SCH - 1)))
            ck_bmm = p.inc("tensor", "s_pe", 1,
                           lambda E, j=j, pair=pair:
                           E.matmul(pair[1], g_fm[:, SCH - 1, j * 128:(j + 1) * 128],
                                    v_sb[:, SCH - 1, 512:1024],
                                    start=False, stop=True))
            p.wait("vector", "s_pe", ck_bmm)
            if j == 0:
                p.wait("vector", "s_dve", r64_done)
            p.emit("vector",
                   lambda E, c=c, pair=pair:
                   E.scalar_tensor_tensor(x_sb[:, c, 0:512], pair[0], r64[:, c:c + 1],
                                          x_sb[:, c, 0:512], ALU.mult, ALU.add))
            ck_bev[j] = p.inc("vector", "s_dve", 1,
                              lambda E, c=c, pair=pair:
                              E.scalar_tensor_tensor(x_sb[:, c, 512:1024], pair[1],
                                                     r64[:, c:c + 1],
                                                     x_sb[:, c, 512:1024],
                                                     ALU.mult, ALU.add))
        prev_bev = {3: ck_bev[3], 6: ck_bev[2], 7: ck_bev[3]}


def emit_tm_to_fm(p, src_tm, dst_fm_writes, B):
    tr_evac = {}
    p.wait("tensor", "s_gp")
    for idx, (j, dst) in enumerate(dst_fm_writes):
        b = B["ps_t0"] if idx % 2 == 0 else B["ps_t1"]
        if idx < 2:
            p.wait("tensor", "s_dve")
        else:
            p.wait("tensor", "s_dve", tr_evac[idx - 2])
        ck = p.inc("tensor", "s_pe", 1,
                   lambda E, j=j, b=b, src_tm=src_tm:
                   E.transpose(b, src_tm[:, j * 128:(j + 1) * 128], B["eye_bf"]))
        p.wait("vector", "s_pe", ck)
        tr_evac[idx] = p.inc("vector", "s_dve", 1,
                             lambda E, dst=dst, b=b: E.tensor_copy(dst, b))
    return tr_evac[len(dst_fm_writes) - 1]


QTHR = (0.25, 0.75, 1.25, 1.75, 2.5, 3.5, 5.0)


def emit_quant(p, src32, dst_bf, N, Q, eng="vector"):
    nb = N // 16
    xn, A, Bb, sgn = Q["xn"], Q["qa"], Q["qb"], Q["sgn"]
    sc, rs, hs = Q["sc"], Q["rs"], Q["hs"]
    esem = "s_dve" if eng == "vector" else "s_gp"
    s3 = lambda ap: ap[:, 0:N].rearrange("p (g k) -> p g k", k=16)
    scb = lambda ap: ap[:, 0:nb].rearrange("p (g o) -> p g o", o=1).broadcast_to([128, nb, 16])
    _V = lambda fn: p.emit(eng, fn)
    D = lambda: p.emit(eng, lambda E: E.drain())

    def V(fn):
        D()
        _V(fn)

    if eng != "vector":
        p.wait(eng, "s_pe")

    p.emit("vector", lambda E: E.drain())
    p.emit("vector", lambda E: E.tensor_reduce(
        sc[:, 0:nb].rearrange("p (g o) -> p g o", o=1),
        s3(src32), AX.X, ALU.max, apply_absolute_value=True))
    p.emit("vector", lambda E: E.drain())
    p.emit("vector", lambda E: E.tensor_scalar(sc[:, 0:nb], sc[:, 0:nb], 1.0 / 6.0,
                                               1e-12, ALU.mult, ALU.max))
    p.emit("vector", lambda E: E.drain())
    p.emit("vector", lambda E: E.tensor_scalar(hs[:, 0:nb], sc[:, 0:nb], 0.5, None,
                                               ALU.mult))
    p.emit("vector", lambda E: E.reciprocal(rs[:, 0:nb], sc[:, 0:nb]))
    ck_rs = p.inc("vector", "s_dve", 1, lambda E: E.drain())
    if eng != "vector":
        p.wait(eng, "s_dve", ck_rs)
    p.wait(eng, "s_act")
    ck_xn = p.inc(eng, esem, 1,
                  lambda E: E.tensor_tensor(s3(xn), s3(src32), scb(rs), ALU.mult))
    D()
    p.wait("scalar", esem, ck_xn)
    ck_sgn = p.inc("scalar", "s_act", 1,
                   lambda E: E.activation(sgn[:, 0:N], xn[:, 0:N], AF.Sign))
    ax = Q["ax"]
    U32 = mybir.dt.uint32
    if eng == "vector":
        V(lambda E: E.tensor_scalar(ax[:, 0:N].bitcast(U32), xn[:, 0:N].bitcast(U32),
                                    0x7FFFFFFF, None, ALU.bitwise_and))
    else:
        p.wait(eng, "s_act", ck_sgn)
        V(lambda E: E.tensor_tensor(ax[:, 0:N], xn[:, 0:N], sgn[:, 0:N], ALU.mult))
    cmp = lambda dst, th: V(lambda E, dst=dst, th=th: E.tensor_scalar(
        dst[:, 0:N], ax[:, 0:N], th, None, ALU.is_gt))
    cmp(A, QTHR[0])
    for th in (QTHR[1], QTHR[2], QTHR[3]):
        cmp(Bb, th)
        D()
        V(lambda E: E.tensor_tensor(A[:, 0:N], A[:, 0:N], Bb[:, 0:N], ALU.add))
    for th, w in ((QTHR[4], 2.0), (QTHR[5], 2.0), (QTHR[6], 4.0)):
        cmp(Bb, th)
        D()
        if eng == "vector":
            V(lambda E, w=w: E.scalar_tensor_tensor(A[:, 0:N], Bb[:, 0:N], w,
                                                    A[:, 0:N], ALU.mult, ALU.add))
        else:
            V(lambda E, w=w: E.tensor_scalar(Bb[:, 0:N], Bb[:, 0:N], w, None, ALU.mult))
            V(lambda E: E.tensor_tensor(A[:, 0:N], A[:, 0:N], Bb[:, 0:N], ALU.add))
    D()
    p.wait(eng, "s_act", ck_sgn)
    V(lambda E: E.tensor_tensor(A[:, 0:N], A[:, 0:N], sgn[:, 0:N], ALU.mult))
    D()
    ck = p.inc(eng, esem, 1,
               lambda E: E.tensor_tensor(s3(dst_bf), s3(A), scb(hs), ALU.mult))
    return (esem, ck)




def emit_wq_tile(p, B, raw_src_ap, wq_dram_dst_ap, stage_buf, ck_stage_sem, ck_stage_val):
    Qw = B["Qw"]
    p.wait("vector", ck_stage_sem, ck_stage_val)
    p.wait("gpsimd", ck_stage_sem, ck_stage_val)
    emit_quant(p, stage_buf, Qw["dst_tm"], 1024, Qw, eng="vector")
    wqc = B["wqc"]
    p.wait("vector", "swq")
    last = emit_tm_to_fm(p, Qw["dst_tm"],
                         [(j, wqc[:, j, :]) for j in range(DCH)], B)
    p.wait("sync", "s_dve", last)
    return p.inc("sync", "swq", 16,
                 lambda E, dst=wq_dram_dst_ap: E.dma_start(out=dst, in_=wqc))


FFN_I = 2816
FFN_STRIPS = [(0, 512), (512, 512), (1024, 512), (1536, 512), (2048, 512), (2560, 256)]


def emit_l0_ffn(p, x_sb, B, wgt_q, wut_q, wd_d):
    bn6, mv, sd, rstd = B["bn6"], B["mv"], B["sd"], B["rstd"]
    QQ = [B["Qv"], B["Qg"]]
    ENG = ["vector", "vector"]
    xq_fm = B["xq_fm"]
    wd_raw = B["wd_raw"]
    wgT, wuT, wdT = B["wgT"], B["wuT"], B["wdT"]
    sg = B["sg"]
    h32 = B["h32"]
    hq_fm = B["hq_fm"]
    ps_g, ps_u = B["ps_a"], B["ps_b"]
    ps_d0, ps_d1 = B["ps_c"], B["ps_d"]
    p.new_sem("sw")
    p.new_sem("swgld")

    wd4 = wd_d.rearrange("(h p) i -> p h i", p=128)

    qck = {}
    for c in range(TCH):
        Qc, eng = QQ[c % 2], ENG[c % 2]
        xc = x_sb[:, c, :]
        p.emit("vector", lambda E, xc=xc: E.bn_stats(bn6[:, 0, :], xc[:, 0:512]))
        p.emit("vector", lambda E, xc=xc: E.bn_stats(bn6[:, 1, :], xc[:, 512:1024]))
        p.emit("vector", lambda E: E.drain())
        p.emit("vector", lambda E: E.bn_aggr(mv, bn6))
        p.emit("vector", lambda E: E.drain())
        p.inc("vector", "s_dve", 1,
              lambda E: E.tensor_scalar(mv[:, 1:2], mv[:, 1:2], LN_EPS, None, ALU.add))
        p.wait("scalar", "s_dve")
        p.inc("scalar", "s_act", 1, lambda E: E.activation(sd, mv[:, 1:2], AF.Sqrt))
        p.wait("vector", "s_act")
        p.emit("vector", lambda E: E.reciprocal(rstd, sd))
        p.emit("vector", lambda E: E.drain())
        if (c - 2) in qck:
            p.wait("vector", qck[c - 2][0], qck[c - 2][1])
        p.inc("vector", "s_dve", 1,
              lambda E, xc=xc, Qc=Qc:
              E.tensor_scalar(Qc["x32"], xc, mv[:, 0:1], rstd, ALU.subtract, ALU.mult))
        qck[c] = emit_quant(p, Qc["x32"], Qc["dst_tm"], 1024, Qc, eng=eng)
        emit_tm_to_fm(p, Qc["dst_tm"],
                      [(j, xq_fm[:, j, c * 128:(c + 1) * 128]) for j in range(DCH)], B)

    prev_w_ready = None
    prev_gu_mm = [None]
    for (i0, W) in FFN_STRIPS:
        nt = W // 128
        r0 = i0 // 128
        if prev_w_ready is not None:
            p.wait("gpsimd", "s_dve", prev_w_ready)
        sw_tot = p.inc("gpsimd", "sw", 16,
                       lambda E, i0=i0, W=W: E.dma_start(out=wd_raw[:, :, 0:W],
                                                         in_=wd4[:, :, i0:i0 + W]))
        if prev_gu_mm[0] is not None:
            p.wait("sync", "s_pe", prev_gu_mm[0])
        p.wait("sync", "swq")
        p.inc("sync", "swgld", 16,
              lambda E, i0=i0, W=W: E.dma_start(
                  out=wgT[:, :, 0:W],
                  in_=wgt_q[:, :, i0:i0 + W].rearrange("kc p i -> p kc i")))
        gu_ld = p.inc("sync", "swgld", 16,
                      lambda E, i0=i0, W=W: E.dma_start(
                          out=wuT[:, :, 0:W],
                          in_=wut_q[:, :, i0:i0 + W].rearrange("kc p i -> p kc i")))
        p.wait("vector", "sw", sw_tot)
        wdck = {}
        for ht in range(8):
            Qc, eng = QQ[ht % 2], ENG[ht % 2]
            if (ht - 2) in wdck:
                pass
            wdck[ht] = emit_quant(p, wd_raw[:, ht, 0:W], Qc["dst_tm"][:, 0:W], W, Qc,
                                  eng=eng)
            emit_tm_to_fm(p, Qc["dst_tm"][:, 0:W],
                          [(ic, wdT[:, ic, ht * 128:(ht + 1) * 128]) for ic in range(nt)],
                          B)
        w_ready = p.counts["s_dve"]
        p.wait("tensor", "s_gp")
        prev_w_ready = w_ready

        ck_evac = {}
        hck = {}
        for c in range(TCH):
            if c == 0:
                p.wait("tensor", "s_dve", w_ready)
                p.wait("tensor", "swgld", gu_ld)
            p.wait("tensor", "s_act")
            for k in range(DCH - 1):
                p.emit("tensor",
                       lambda E, k=k, c=c, W=W:
                       E.matmul(ps_g[:, 0:W], xq_fm[:, k, c * 128:(c + 1) * 128],
                                wgT[:, k, 0:W], start=(k == 0), stop=False))
            ck_g = p.inc("tensor", "s_pe", 1,
                         lambda E, c=c, W=W:
                         E.matmul(ps_g[:, 0:W], xq_fm[:, DCH - 1, c * 128:(c + 1) * 128],
                                  wgT[:, DCH - 1, 0:W], start=False, stop=True))
            for k in range(DCH - 1):
                p.emit("tensor",
                       lambda E, k=k, c=c, W=W:
                       E.matmul(ps_u[:, 0:W], xq_fm[:, k, c * 128:(c + 1) * 128],
                                wuT[:, k, 0:W], start=(k == 0), stop=False))
            ck_u = p.inc("tensor", "s_pe", 1,
                         lambda E, c=c, W=W:
                         E.matmul(ps_u[:, 0:W], xq_fm[:, DCH - 1, c * 128:(c + 1) * 128],
                                  wuT[:, DCH - 1, 0:W], start=False, stop=True))
            Qc, eng = QQ[c % 2], ENG[c % 2]
            p.wait("scalar", "s_pe", ck_g)
            p.wait("scalar", "s_dve")
            ck_silu = p.inc("scalar", "s_act", 1,
                            lambda E, W=W, c=c:
                            E.activation(sg[:, c % 2, 0:W], ps_g[:, 0:W], AF.Silu))
            p.wait("vector", "s_act", ck_silu)
            p.wait("vector", "s_pe", ck_u)
            if (c - 2) in hck:
                p.wait("vector", hck[c - 2][0], hck[c - 2][1])
            p.inc("vector", "s_dve", 1,
                  lambda E, W=W, c=c:
                  E.tensor_tensor(h32[:, c % 2, 0:W], sg[:, c % 2, 0:W], ps_u[:, 0:W],
                                  ALU.mult))
            hck[c] = emit_quant(p, h32[:, c % 2, 0:W], Qc["dst_tm"][:, 0:W], W, Qc,
                                eng=eng)
            hq_ready = emit_tm_to_fm(p, Qc["dst_tm"][:, 0:W],
                                     [(ic, hq_fm[:, ic, :]) for ic in range(nt)], B)
            p.wait("tensor", "s_dve", hq_ready)
            for hh, psd in ((0, ps_d0), (1, ps_d1)):
                if c >= 1:
                    p.wait("tensor", "s_dve", ck_evac[(c - 1, hh)])
                for ic in range(nt - 1):
                    p.emit("tensor",
                           lambda E, ic=ic, hh=hh:
                           E.matmul(psd if False else (ps_d0 if hh == 0 else ps_d1),
                                    hq_fm[:, ic, :], wdT[:, ic, hh * 512:(hh + 1) * 512],
                                    start=(ic == 0), stop=False))
                ck_d = p.inc("tensor", "s_pe", 1,
                             lambda E, hh=hh, nt=nt:
                             E.matmul(ps_d0 if hh == 0 else ps_d1,
                                      hq_fm[:, nt - 1, :],
                                      wdT[:, nt - 1, hh * 512:(hh + 1) * 512],
                                      start=(nt == 1), stop=True))
                p.wait("vector", "s_pe", ck_d)
                ck_evac[(c, hh)] = p.inc(
                    "vector", "s_dve", 1,
                    lambda E, c=c, hh=hh:
                    E.tensor_tensor(x_sb[:, c, hh * 512:(hh + 1) * 512],
                                    x_sb[:, c, hh * 512:(hh + 1) * 512],
                                    ps_d0 if hh == 0 else ps_d1, ALU.add))
        prev_gu_mm[0] = p.counts["s_pe"]


def emit_l1_attn(p, x_sb, B, wqkv_d, wo_d, skip_cc=False):
    nc = p.nc
    Q = B["Q"]
    bn6, mv, sd, rstd = B["bn6"], B["mv"], B["sd"], B["rstd"]
    xln_t = B["xln_t"]
    xr = B["xr"]
    q_ch = B["q_ch"]
    q_ld = [B["q_ld0"], B["q_ld1"]]
    q_dram = nc.dram_tensor("q_spill", [TCH, 128, 1024], BF16, kind="Internal").ap()
    kn_tm = B["kn_tm"]
    v_tm = B["v_tm"]
    y_fm = B["y_fm"]
    wt = [B["wt0"], B["wt1"]]
    ks32 = Q["xn"]
    sq32 = B["sq_bf"]
    st16 = B["st16"]
    st16b = B["st16b"]
    q_rot = B["q_rot"]
    kn_rot = B["kn_rot"]
    scm = [B["scm0"], B["scm1"]]
    ST32 = B["ST32"]
    STbf = B["STbf"]
    T_sb = B["T_sb"]
    cc_mask = B["cc_mask"]
    maskT = B["maskT"]
    ps4 = [B["ps_a"], B["ps_b"], B["ps_c"], B["ps_d"]]
    ps_ab = [B["ps_a"], B["ps_b"]]
    ps_y = B["ps_c"]
    ps_st = [B["ps_e"], B["ps_f"]]
    HP = 16

    for s in ("sqkv0", "sqkv1", "scc", "sqs0", "sqs1", "sql0", "sql1"):
        p.new_sem(s)

    wq4 = wqkv_d.rearrange("(kc p) n -> p kc n", p=128)
    wo4 = wo_d.rearrange("(kc p) n -> p kc n", p=128)

    ck_ev = {}
    ck_lnk = {}
    nld = [0]

    def load_wtile(src_ap, wait_pe=None):
        i = nld[0]
        nld[0] += 1
        sem = "sqkv%d" % (i % 2)
        if wait_pe is not None:
            p.wait("gpsimd", "s_pe", wait_pe)
        ck = p.inc("gpsimd", sem, 16,
                   lambda E, src_ap=src_ap, i=i: E.dma_start(out=wt[i % 2], in_=src_ap))
        return sem, ck, wt[i % 2]

    ck_mmgrp = {}
    for g in range(4):
        cs = [g * 4 + i for i in range(4)]
        for ci, c in enumerate(cs):
            xc = x_sb[:, c, :]
            p.emit("vector", lambda E, xc=xc: E.bn_stats(bn6[:, 0, :], xc[:, 0:512]))
            p.emit("vector", lambda E, xc=xc: E.bn_stats(bn6[:, 1, :], xc[:, 512:1024]))
            p.emit("vector", lambda E: E.drain())
            p.emit("vector", lambda E: E.bn_aggr(mv, bn6))
            p.emit("vector", lambda E: E.drain())
            p.inc("vector", "s_dve", 1,
                  lambda E: E.tensor_scalar(mv[:, 1:2], mv[:, 1:2], LN_EPS, None, ALU.add))
            p.wait("scalar", "s_dve")
            p.inc("scalar", "s_act", 1, lambda E: E.activation(sd, mv[:, 1:2], AF.Sqrt))
            p.wait("vector", "s_act")
            p.emit("vector", lambda E: E.reciprocal(rstd, sd))
            p.emit("vector", lambda E: E.drain())
            p.inc("vector", "s_dve", 1,
                  lambda E, xc=xc:
                  E.tensor_scalar(xln_t, xc, mv[:, 0:1], rstd, ALU.subtract, ALU.mult))
            for j in range(DCH):
                b = B["ps_t0"] if j % 2 == 0 else B["ps_t1"]
                p.wait("tensor", "s_dve")
                ck = p.inc("tensor", "s_pe", 1,
                           lambda E, j=j, b=b:
                           E.transpose(b, xln_t[:, j * 128:(j + 1) * 128], B["eye_bf"]))
                p.wait("vector", "s_pe", ck)
                p.inc("vector", "s_dve", 1,
                      lambda E, ci=ci, j=j, b=b:
                      E.tensor_copy(xr[:, ci, j, :], b))
        xr_ready = p.counts["s_dve"]
        p.wait("tensor", "s_dve", xr_ready)
        for cg in range(6):
            for k in range(DCH):
                sem, ck_w, wtile = load_wtile(
                    wq4[:, k, cg * 512:(cg + 1) * 512],
                    wait_pe=ck_mmgrp.get(nld[0] - 2))
                p.wait("tensor", sem, ck_w)
                for ci in range(4):
                    prev_use = (cs[ci], cg - 1) if cg >= 1 else (cs[ci] - 4, 5)
                    if k == 0 and prev_use in ck_ev:
                        p.wait("tensor", "s_dve", ck_ev[prev_use])
                    if ci < 3:
                        p.emit("tensor",
                               lambda E, ci=ci, k=k, wtile=wtile:
                               E.matmul(ps4[ci], xr[:, ci, k, :], wtile,
                                        start=(k == 0), stop=(k == DCH - 1)))
                    else:
                        ck_mmgrp[nld[0] - 1] = p.inc(
                            "tensor", "s_pe", 1,
                            lambda E, ci=ci, k=k, wtile=wtile:
                            E.matmul(ps4[ci], xr[:, ci, k, :], wtile,
                                     start=(k == 0), stop=(k == DCH - 1)))
            mm_done = p.counts["s_pe"]
            for ci, c in enumerate(cs):
                p.wait("vector", "s_pe", mm_done)
                if cg < 2:
                    if cg == 0 and c >= 4:
                        p.wait("vector", "sqs%d" % ((c // 4 - 1) % 2))
                    dst = q_ch[:, ci, cg * 512:(cg + 1) * 512]
                elif cg < 4:
                    dst = kn_tm[:, c, (cg - 2) * 512:(cg - 1) * 512]
                else:
                    dst = v_tm[:, c, (cg - 4) * 512:(cg - 3) * 512]
                ck_ev[(c, cg)] = p.inc("vector", "s_dve", 1,
                                       lambda E, dst=dst, ci=ci:
                                       E.tensor_copy(dst, ps4[ci]))
        for ci, c in enumerate(cs):
            p.wait("sync", "s_dve", ck_ev[(c, 1)])
            p.inc("sync", "sqs%d" % ((c // 4) % 2), 16,
                  lambda E, c=c, ci=ci: E.dma_start(out=q_dram[c], in_=q_ch[:, ci, :]))
            k3 = kn_tm[:, c, :].rearrange("p (h d) -> p h d", d=64)
            D = lambda: p.emit("vector", lambda E: E.drain())
            D()
            p.emit("vector", lambda E, k3=k3: E.tensor_reduce(
                st16.rearrange("p (h o) -> p h o", o=1), k3, AX.X, ALU.add))
            D()
            p.emit("vector", lambda E: E.tensor_scalar(st16, st16, 1.0 / 64, None, ALU.mult))
            D()
            p.emit("vector", lambda E, k3=k3: E.tensor_tensor(
                ks32.rearrange("p (h d) -> p h d", d=64), k3,
                st16.rearrange("p (h o) -> p h o", o=1).broadcast_to([128, 16, 64]),
                ALU.subtract))
            D()
            p.emit("vector", lambda E: E.tensor_tensor(sq32, ks32, ks32, ALU.mult))
            D()
            p.emit("vector", lambda E: E.tensor_reduce(
                st16b.rearrange("p (h o) -> p h o", o=1),
                sq32.rearrange("p (h d) -> p h d", d=64), AX.X, ALU.add))
            D()
            p.inc("vector", "s_dve", 1,
                  lambda E: E.tensor_scalar(st16b, st16b, 1.0 / 64, LN_EPS,
                                            ALU.mult, ALU.add))
            p.wait("scalar", "s_dve")
            p.inc("scalar", "s_act", 1, lambda E: E.activation(st16b, st16b, AF.Sqrt))
            p.wait("vector", "s_act")
            p.emit("vector", lambda E: E.reciprocal(st16b, st16b))
            D()
            ck_lnk[c] = p.inc("vector", "s_dve", 1,
                              lambda E, c=c: E.tensor_tensor(
                                  kn_tm[:, c, :].rearrange("p (h d) -> p h d", d=64),
                                  ks32.rearrange("p (h d) -> p h d", d=64),
                                  st16b.rearrange("p (h o) -> p h o", o=1)
                                  .broadcast_to([128, 16, 64]),
                                  ALU.mult))
    qkv_done = p.counts["s_dve"]

    p.wait("tensor", "s_dve", qkv_done)
    for h in range(HP):
        ph, fs = (h % 2) * 64, h // 2
        for c in range(TCH):
            fn = (lambda E, h=h, c=c, ph=ph, fs=fs:
                  E.matmul(ps_st[0][ph:ph + 64, fs, :],
                           kn_tm[:, c, h * 64:(h + 1) * 64],
                           v_tm[:, c, h * 64:(h + 1) * 64],
                           start=(c == 0), stop=(c == TCH - 1)))
            if h == HP - 1 and c == TCH - 1:
                ck_T = p.inc("tensor", "s_pe", 1, fn)
            else:
                p.emit("tensor", fn)
    p.wait("vector", "s_pe", ck_T)
    p.emit("vector", lambda E: E.tensor_copy(
        T_sb.rearrange("p a b -> p (a b)"), ps_st[0].rearrange("p a b -> p (a b)")))
    ck_Tsb = p.inc("vector", "s_dve", 1, lambda E: E.drain())

    t_loc = nc.dram_tensor("t_loc", [1024, 64], F32, kind="Internal").ap()
    t_all = nc.dram_tensor("t_all", [8192, 64], F32, kind="Internal").ap()
    p.wait("sync", "s_dve", ck_Tsb)
    ck_tdma = p.inc("sync", "s_ld", 16,
                    lambda E: E.dma_start(
                        out=t_loc.rearrange("(fs ph pe) fd -> (ph pe) fs fd", ph=2, pe=64),
                        in_=T_sb))
    if not skip_cc:
        p.wait("gpsimd", "s_ld", ck_tdma)
        p.inc("gpsimd", "scc", 1,
              lambda E: E.collective_compute(
                  "AllGather", ALU.bypass,
                  replica_groups=[[0, 1, 2, 3, 4, 5, 6, 7]],
                  ins=[t_loc.opt()],
                  outs=[t_all.opt()]))
        cc_done = p.counts["scc"]
    p.emit("vector", lambda E: E.memset(ST32.rearrange("p a b -> p (a b)"), 0.0))
    p.emit("vector", lambda E: E.drain())
    if not skip_cc:
        tr_sb = B["tr_sb"]
        for r in range(8):
            p.wait("sync", "scc", cc_done)
            if r >= 1:
                p.wait("sync", "s_dve")
            ck_r = p.inc("sync", "s_ld", 16,
                         lambda E, r=r: E.dma_start(
                             out=tr_sb,
                             in_=t_all[r * 1024:(r + 1) * 1024, :]
                             .rearrange("(fs ph pe) fd -> (ph pe) fs fd", ph=2, pe=64)))
            p.wait("vector", "s_ld", ck_r)
            p.inc("vector", "s_dve", 1,
                  lambda E, r=r: E.scalar_tensor_tensor(
                      ST32.rearrange("p a b -> p (a b)"),
                      tr_sb.rearrange("p a b -> p (a b)"),
                      cc_mask[:, r:r + 1],
                      ST32.rearrange("p a b -> p (a b)"), ALU.mult, ALU.add))
            p.emit("vector", lambda E: E.drain())
    p.emit("vector", lambda E: E.tensor_copy(
        STbf.rearrange("p a b -> p (a b)"), ST32.rearrange("p a b -> p (a b)")))
    st_ready = p.inc("vector", "s_dve", 1, lambda E: E.drain())

    ck_cast = {-1: st_ready}
    ck_stadd = {}
    prev_yev = {}
    ck_qld = {}
    ck_qtr = {}
    p.wait("sync", "sqs0")
    p.wait("sync", "sqs1")
    ck_qld[0] = p.inc("sync", "sql0", 16, lambda E: E.dma_start(out=q_ld[0], in_=q_dram[0]))
    for c in range(TCH):
        if c + 1 < TCH:
            if (c - 1) in ck_qtr:
                p.wait("sync", "s_pe", ck_qtr[c - 1])
            ck_qld[c + 1] = p.inc("sync", "sql%d" % ((c + 1) % 2), 16,
                                  lambda E, c=c: E.dma_start(out=q_ld[(c + 1) % 2],
                                                             in_=q_dram[c + 1]))
        p.wait("tensor", "sql%d" % (c % 2), ck_qld[c])
        last_ev = None
        last_qtr = None
        for (srcname, dstrot) in (("q", q_rot), ("kn", kn_rot)):
            for j in range(8):
                b = B["ps_t0"] if j % 2 == 0 else B["ps_t1"]
                p.wait("tensor", "s_dve")
                if srcname == "q":
                    ck = p.inc("tensor", "s_pe", 1,
                               lambda E, c=c, j=j, b=b:
                               E.transpose(b, q_ld[c % 2][:, j * 128:(j + 1) * 128],
                                           B["eye_bf"]))
                    last_qtr = ck
                else:
                    ck = p.inc("tensor", "s_pe", 1,
                               lambda E, c=c, j=j, b=b:
                               E.transpose(b, kn_tm[:, c, j * 128:(j + 1) * 128],
                                           B["eye_bf"]))
                p.wait("vector", "s_pe", ck)
                last_ev = p.inc("vector", "s_dve", 1,
                                lambda E, dstrot=dstrot, j=j, b=b:
                                E.tensor_copy(dstrot[:, j, :], b))
        ck_qtr[c] = last_qtr
        p.wait("tensor", "s_dve", last_ev)
        p.wait("tensor", "s_dve", ck_cast[c - 1])
        ck_scev = {}
        ck_ymm = {}
        for h in range(HP):
            ph, fs = (h % 2) * 64, h // 2
            bank = ps_ab[h % 2]
            if h >= 2:
                p.wait("tensor", "s_dve", ck_scev[h - 2])
            ck_sc = p.inc("tensor", "s_pe", 1,
                          lambda E, h=h, ph=ph, fs=fs, bank=bank:
                          E.matmul(bank[:, 0:128], kn_rot[ph:ph + 64, fs, :],
                                   q_rot[ph:ph + 64, fs, :], start=True, stop=True))
            p.wait("vector", "s_pe", ck_sc)
            ck_scev[h] = p.inc("vector", "s_dve", 1,
                               lambda E, h=h, bank=bank:
                               E.tensor_tensor(scm[h % 2], bank[:, 0:128], maskT,
                                               ALU.mult))
            p.wait("tensor", "s_dve", ck_scev[h])
            if h % 2 == 0:
                _pk = h // 2 - 1 if h >= 2 else 7
                if _pk in prev_yev:
                    p.wait("tensor", "s_dve", prev_yev[_pk])
            p.emit("tensor",
                   lambda E, h=h, c=c, ph=ph:
                   E.matmul(ps_y[ph:ph + 64, 0:128], v_tm[:, c, h * 64:(h + 1) * 64],
                            scm[h % 2], start=True, stop=False))
            ck_ymm[h] = p.inc("tensor", "s_pe", 1,
                              lambda E, h=h, ph=ph, fs=fs:
                              E.matmul(ps_y[ph:ph + 64, 0:128], STbf[ph:ph + 64, fs, :],
                                       q_rot[ph:ph + 64, fs, :], start=False, stop=True))
            if h % 2 == 1:
                p.wait("vector", "s_pe", ck_ymm[h])
                prev_yev[h // 2] = p.inc(
                    "vector", "s_dve", 1,
                    lambda E, c=c, j=h // 2:
                    E.tensor_copy(y_fm[:, j, c * 128:(c + 1) * 128], ps_y[:, 0:128]))
            if h == 0:
                if c == 0:
                    p.wait("tensor", "s_dve", ck_Tsb)
                elif (c - 2) in ck_stadd:
                    p.wait("tensor", "s_dve", ck_stadd[c - 2])
            fn_st = (lambda E, h=h, c=c, ph=ph, fs=fs:
                     E.matmul(ps_st[c % 2][ph:ph + 64, fs, :],
                              kn_tm[:, c, h * 64:(h + 1) * 64],
                              v_tm[:, c, h * 64:(h + 1) * 64], start=True, stop=True))
            if h == HP - 1:
                ck_allst = p.inc("tensor", "s_pe", 1, fn_st)
            else:
                p.emit("tensor", fn_st)

        p.wait("vector", "s_pe", ck_allst)
        ck_stadd[c] = p.inc("vector", "s_dve", 1,
                            lambda E, c=c: E.tensor_tensor(
                                ST32.rearrange("p a b -> p (a b)"),
                                ST32.rearrange("p a b -> p (a b)"),
                                ps_st[c % 2].rearrange("p a b -> p (a b)"), ALU.add))
        p.emit("vector", lambda E: E.drain())
        ck_cast[c] = p.inc("vector", "s_dve", 1,
                           lambda E: E.tensor_copy(STbf.rearrange("p a b -> p (a b)"),
                                                   ST32.rearrange("p a b -> p (a b)")))

    y_done = max(prev_yev.values())
    p.wait("tensor", "s_dve", y_done)
    ck_oev = {}
    ck_omm = {}
    for g in range(4):
        cs = [g * 4 + i for i in range(4)]
        for og in range(2):
            for k in range(DCH):
                sem, ck_w, wtile = load_wtile(
                    wo4[:, k, og * 512:(og + 1) * 512],
                    wait_pe=ck_mmgrp.get(nld[0] - 2))
                p.wait("tensor", sem, ck_w)
                for ci in range(4):
                    prev_use = (cs[ci], og - 1) if og >= 1 else (cs[ci] - 4, 1)
                    if k == 0 and prev_use in ck_oev:
                        p.wait("tensor", "s_dve", ck_oev[prev_use])
                    if ci < 3:
                        p.emit("tensor",
                               lambda E, ci=ci, k=k, wtile=wtile, cc=cs[ci]:
                               E.matmul(ps4[ci], y_fm[:, k, cc * 128:(cc + 1) * 128],
                                        wtile, start=(k == 0), stop=(k == DCH - 1)))
                    else:
                        ck_mmgrp[nld[0] - 1] = p.inc(
                            "tensor", "s_pe", 1,
                            lambda E, ci=ci, k=k, wtile=wtile, cc=cs[ci]:
                            E.matmul(ps4[ci], y_fm[:, k, cc * 128:(cc + 1) * 128],
                                     wtile, start=(k == 0), stop=(k == DCH - 1)))
            mm_done = p.counts["s_pe"]
            for ci, c in enumerate(cs):
                p.wait("vector", "s_pe", mm_done)
                ck_oev[(c, og)] = p.inc(
                    "vector", "s_dve", 1,
                    lambda E, c=c, og=og, ci=ci:
                    E.tensor_tensor(x_sb[:, c, og * 512:(og + 1) * 512],
                                    x_sb[:, c, og * 512:(og + 1) * 512],
                                    ps4[ci], ALU.add))


def emit_moe(p, x_sb, xln_fm, B, rwt_d, w1t_d, w2t_d, b1r_d, b2_d):
    nc = p.nc
    lg = B["lg"]
    top8 = B["top8"]
    m1, m2, s1, s2 = B["m1"], B["m2"], B["s1"], B["s2"]
    i1, i2 = B["i1"], B["i2"]
    route_tm = B["route_tm"]
    route_bf = B["route_bf"]
    route_fm = B["route_fm"]
    rwt_sb = B["rwt_sb"]
    b2_sb = B["b2_sb"]
    b1c = B["b1c"]
    h1g = B["h1g"]
    w2q = B["w2q"]
    w1s = [B["w1s0"], B["w1s1"]]
    ps_ab = [B["ps_a"], B["ps_b"]]
    ps_cd = [B["ps_c"], B["ps_d"]]
    ps_e = B["ps_e"]
    ps_e8 = ps_e.rearrange("p a b -> p (a b)")[:, 0:8] if hasattr(ps_e, "rearrange") else ps_e

    for s in ("sw1a", "sw1b", "sw2", "sb1", "smoe"):
        p.new_sem(s)

    w14 = w1t_d.rearrange("e (kc p) f -> e p kc f", p=128)
    w24 = w2t_d.rearrange("e (fc p) h -> e p fc h", p=128)

    p.inc("gpsimd", "smoe", 16,
          lambda E: E.dma_start(out=rwt_sb, in_=rwt_d.rearrange("(kc p) e -> p kc e", p=128)))
    p.inc("gpsimd", "smoe", 16, lambda E: E.dma_start(out=b2_sb, in_=b2_d))
    rw_ready = p.counts["smoe"]

    p.wait("tensor", "s_dve")
    p.wait("tensor", "smoe", rw_ready)
    ck_rt = {}
    for c in range(TCH):
        if c >= 1:
            p.wait("tensor", "s_dve", ck_rt[c - 1])
        for k in range(DCH - 1):
            p.emit("tensor",
                   lambda E, c=c, k=k:
                   E.matmul(ps_e8, xln_fm[:, k, c * 128:(c + 1) * 128],
                            rwt_sb[:, k, :], start=(k == 0), stop=False))
        ck_l = p.inc("tensor", "s_pe", 1,
                     lambda E, c=c:
                     E.matmul(ps_e8, xln_fm[:, DCH - 1, c * 128:(c + 1) * 128],
                              rwt_sb[:, DCH - 1, :], start=False, stop=True))
        p.wait("vector", "s_pe", ck_l)
        p.emit("vector", lambda E: E.tensor_copy(lg, ps_e8))
        ck_rt[c] = p.inc("vector", "s_dve", 1, lambda E: E.drain())
        p.emit("vector", lambda E: E.max(top8, lg))
        p.emit("vector", lambda E: E.drain())
        p.emit("vector", lambda E: E.tensor_copy(m1, top8[:, 0:1]))
        p.emit("vector", lambda E: E.tensor_copy(m2, top8[:, 1:2]))
        p.emit("vector", lambda E: E.drain())
        p.inc("vector", "s_dve", 1,
              lambda E: E.tensor_tensor(s1, m1, m2, ALU.subtract))
        p.wait("scalar", "s_dve")
        p.inc("scalar", "s_act", 1, lambda E: E.activation(s1, s1, AF.Sigmoid))
        p.wait("vector", "s_act")
        p.emit("vector", lambda E: E.tensor_scalar(s2, s1, -1.0, 1.0, ALU.mult, ALU.add))
        p.emit("vector", lambda E: E.drain())
        p.emit("vector", lambda E: E.tensor_scalar(i1, lg, m1, None, ALU.is_equal))
        p.emit("vector", lambda E: E.drain())
        p.emit("vector", lambda E: E.tensor_scalar(i2, lg, m2, None, ALU.is_equal))
        p.emit("vector", lambda E: E.drain())
        p.emit("vector", lambda E: E.tensor_scalar(i1, i1, s1, None, ALU.mult))
        p.emit("vector", lambda E: E.drain())
        p.emit("vector", lambda E: E.tensor_scalar(i2, i2, s2, None, ALU.mult))
        p.emit("vector", lambda E: E.drain())
        p.emit("vector", lambda E, c=c: E.tensor_tensor(route_tm[:, c, :], i1, i2, ALU.add))
        p.emit("vector", lambda E: E.drain())
        p.inc("vector", "s_dve", 1,
              lambda E, c=c: E.tensor_copy(route_bf, route_tm[:, c, :]))
        p.wait("tensor", "s_dve")
        ck_tr = p.inc("tensor", "s_pe", 1,
                      lambda E: E.transpose(B["ps_t0"][0:8, 0:128], route_bf, B["eye_bf"]))
        p.wait("vector", "s_pe", ck_tr)
        p.inc("vector", "s_dve", 1,
              lambda E, c=c: E.tensor_copy(route_fm[:, c * 128:(c + 1) * 128],
                                           B["ps_t0"][0:8, 0:128]))
    route_done = p.counts["s_dve"]

    p.wait("tensor", "s_dve", route_done)
    ck_b2 = {}
    for c in range(TCH):
        for hg in range(2):
            if (c - 1, hg) in ck_b2:
                p.wait("tensor", "s_dve", ck_b2[(c - 1, hg)])
            ck = p.inc("tensor", "s_pe", 1,
                       lambda E, c=c, hg=hg:
                       E.matmul(ps_cd[hg], route_fm[:, c * 128:(c + 1) * 128],
                                b2_sb[:, hg * 512:(hg + 1) * 512],
                                start=True, stop=True))
            p.wait("vector", "s_pe", ck)
            ck_b2[(c, hg)] = p.inc("vector", "s_dve", 1,
                                   lambda E, c=c, hg=hg:
                                   E.tensor_tensor(x_sb[:, c, hg * 512:(hg + 1) * 512],
                                                   x_sb[:, c, hg * 512:(hg + 1) * 512],
                                                   ps_cd[hg], ALU.add))

    nw1 = [0]
    ck_w1mm = {}

    def load_w1(e, fq, fc):
        i = nw1[0]
        nw1[0] += 1
        sem = "sw1a" if i % 2 == 0 else "sw1b"
        if i >= 2 and (i - 2) in ck_w1mm:
            p.wait("gpsimd", "s_pe", ck_w1mm[i - 2])
        f0 = fq * 1024 + fc * 128
        ck = p.inc("gpsimd", sem, 16,
                   lambda E, e=e, f0=f0, i=i:
                   E.dma_start(out=w1s[i % 2], in_=w14[e][:, :, f0:f0 + 128]))
        return sem, ck, w1s[i % 2]

    ck_b1 = {}
    ck_w2ph = {}
    prev_h1_last = [0]
    ck_oev = dict(ck_b2)
    for e in range(8):
        p.wait("gpsimd", "s_act")
        ck_b1[e] = p.inc("gpsimd", "sb1", 16,
                         lambda E, e=e: E.dma_start(out=b1c[:, e % 2, :], in_=b1r_d[e]))
        for fq in range(4):
            if (e, fq - 1) in ck_w2ph:
                p.wait("gpsimd", "s_pe", ck_w2ph[(e, fq - 1)])
            elif (e - 1, 3) in ck_w2ph:
                p.wait("gpsimd", "s_pe", ck_w2ph[(e - 1, 3)])
            ck_w2 = p.inc("gpsimd", "sw2", 16,
                          lambda E, e=e, fq=fq:
                          E.dma_start(out=w2q, in_=w24[e][:, fq * 8:(fq + 1) * 8, :]))
            ck_h1ev = {}
            for fc in range(8):
                sem, ck_w, wtile = load_w1(e, fq, fc)
                p.wait("tensor", sem, ck_w)
                for tt in range(4):
                    gidx = fc * 4 + tt
                    bank = ps_ab[gidx % 2]
                    if (gidx - 2) in ck_h1ev:
                        p.wait("tensor", "s_act", ck_h1ev[gidx - 2])
                    elif (e, fq) != (0, 0) and gidx < 2:
                        p.wait("tensor", "s_act", prev_h1_last[0])
                    for k in range(DCH - 1):
                        p.emit("tensor",
                               lambda E, k=k, tt=tt, wtile=wtile, bank=bank:
                               E.matmul(bank, wtile[:, k, :],
                                        xln_fm[:, k, tt * 512:(tt + 1) * 512],
                                        start=(k == 0), stop=False))
                    ck_h1 = p.inc("tensor", "s_pe", 1,
                                  lambda E, tt=tt, wtile=wtile, bank=bank:
                                  E.matmul(bank, wtile[:, DCH - 1, :],
                                           xln_fm[:, DCH - 1, tt * 512:(tt + 1) * 512],
                                           start=False, stop=True))
                    ck_w1mm[nw1[0] - 1] = ck_h1
                    p.wait("scalar", "s_pe", ck_h1)
                    if (e, fq) == (0, 0) and fc == 0 and tt == 0:
                        p.wait("scalar", "sb1", ck_b1[e])
                    elif tt == 0 and fc == 0:
                        p.wait("scalar", "sb1", ck_b1[e])
                    ck_h1ev[gidx] = p.inc(
                        "scalar", "s_act", 1,
                        lambda E, e=e, fq=fq, fc=fc, tt=tt, bank=bank:
                        E.activation(h1g[:, fc, tt * 512:(tt + 1) * 512], bank,
                                     AF.Gelu, bias=b1c[:, e % 2, fq * 8 + fc:fq * 8 + fc + 1]))
            h1_done = p.counts["s_act"]
            prev_h1_last[0] = h1_done
            p.wait("tensor", "s_act", h1_done)
            p.wait("tensor", "sw2", ck_w2)
            last_mm = None
            for c in range(TCH):
                for hg in range(2):
                    bank = ps_cd[(c * 2 + hg) % 2]
                    prev = (c - 1, hg) if c >= 1 else (TCH - 1, hg)
                    if prev in ck_oev:
                        p.wait("tensor", "s_dve", ck_oev[prev])
                    for fc in range(7):
                        p.emit("tensor",
                               lambda E, c=c, hg=hg, fc=fc, bank=bank:
                               E.matmul(bank, h1g[:, fc, c * 128:(c + 1) * 128],
                                        w2q[:, fc, hg * 512:(hg + 1) * 512],
                                        start=(fc == 0), stop=False))
                    last_mm = p.inc("tensor", "s_pe", 1,
                                    lambda E, c=c, hg=hg, bank=bank:
                                    E.matmul(bank, h1g[:, 7, c * 128:(c + 1) * 128],
                                             w2q[:, 7, hg * 512:(hg + 1) * 512],
                                             start=False, stop=True))
                    p.wait("vector", "s_pe", last_mm)
                    ck_oev[(c, hg)] = p.inc(
                        "vector", "s_dve", 1,
                        lambda E, e=e, c=c, hg=hg, bank=bank:
                        E.scalar_tensor_tensor(x_sb[:, c, hg * 512:(hg + 1) * 512],
                                               bank, route_tm[:, c, e:e + 1],
                                               x_sb[:, c, hg * 512:(hg + 1) * 512],
                                               ALU.mult, ALU.add))
            ck_w2ph[(e, fq)] = last_mm


U8 = mybir.dt.uint8
ARENA_BYTES = 140 * 1024


def _carve_factory(arena):
    state = {"off": 0}

    def carve(shape, dtype, off=None):
        esz = {F32: 4, BF16: 2}[dtype]
        n = 1
        for s in shape[1:]:
            n *= s
        nbytes = n * esz
        if off is None:
            off = state["off"]
        state["off"] = off + ((nbytes + 63) // 64) * 64
        v = arena[:, off:off + nbytes].bitcast(dtype)
        if len(shape) == 2:
            return v
        if len(shape) == 3:
            return v.rearrange("p (a b) -> p a b", a=shape[1])
        if len(shape) == 4:
            return v.rearrange("p (a b c) -> p a b c", a=shape[1], b=shape[2])
        raise ValueError(shape)

    def reset():
        state["off"] = 0
    return carve, reset


def build_fused(skip_cc=False):
    nc = bass.Bass("TRN2", target_bir_lowering=False, debug=False)
    d = lambda n, s: nc.dram_tensor(n, s, F32, kind="ExternalInput").ap()
    x_d = d("x", [2048, 1024])
    kt_d = d("kt", [1024, 4096])
    v_d = d("v", [4096, 1024])
    wg_d = d("wg", [2816, 1024])
    wu_d = d("wu", [2816, 1024])
    wd_d = d("wd", [1024, 2816])
    wqkv_d = d("wqkv", [1024, 3072])
    wo_d = d("wo", [1024, 1024])
    rwt_d = d("rwt", [1024, 8])
    w1t_d = d("w1t", [8, 1024, 4096])
    w2t_d = d("w2t", [8, 4096, 1024])
    b1r_d = d("b1r", [8, 128, 32])
    b2_d = d("b2", [8, 1024])
    eye_d = d("eye", [128, 128])
    ones_d = d("ones", [128, 1])
    maskt_d = d("maskt", [128, 128])
    ccm_d = d("ccm", [128, 8])
    xo_d = nc.dram_tensor("xo", [2048, 1024], F32, kind="ExternalOutput").ap()

    a = lambda n, s, dt: nc.alloc_sbuf_tensor(n, s, dt).ap()
    x_sb = a("x_sb", [128, 16, 1024], F32)
    com = {
        "bn6": a("bn6", [128, 2, 6], F32),
        "mv": a("mv", [128, 2], F32),
        "sd": a("sd", [128, 1], F32),
        "rstd": a("rstd", [128, 1], F32),
        "eye_bf": a("eye_bf", [128, 128], BF16),
        "xln_t": a("xln_t", [128, 1024], BF16),
        "ones_bf": a("ones_bf", [128, 1], BF16),
        "ones_f32": a("ones_f32", [128, 1], F32),
        "cc_mask": a("cc_mask", [128, 8], F32),
        "maskT": a("maskT", [128, 128], BF16),
        "ps_t0": nc.alloc_psum_tensor("ps_t0", [128, 128], BF16).ap(),
        "ps_t1": nc.alloc_psum_tensor("ps_t1", [128, 128], BF16).ap(),
        "ps_a": nc.alloc_psum_tensor("ps_a", [128, 512], F32).ap(),
        "ps_b": nc.alloc_psum_tensor("ps_b", [128, 512], F32).ap(),
        "ps_c": nc.alloc_psum_tensor("ps_c", [128, 512], F32).ap(),
        "ps_d": nc.alloc_psum_tensor("ps_d", [128, 512], F32).ap(),
        "ps_e": nc.alloc_psum_tensor("ps_e", [128, 8, 64], F32).ap(),
        "ps_f": nc.alloc_psum_tensor("ps_f", [128, 8, 64], F32).ap(),
    }
    arena = nc.alloc_sbuf_tensor("arena", [128, ARENA_BYTES], U8).ap()
    carve, reset = _carve_factory(arena)

    p = Prog(nc)
    p.inc("sync", "s_ld", 16,
          lambda E: E.dma_start(out=x_sb, in_=x_d.rearrange("(c p) d -> p c d", p=128)))
    p.inc("gpsimd", "s_ldg", 16, lambda E: E.dma_start(out=com["eye_bf"], in_=eye_d))
    p.inc("gpsimd", "s_ldg", 16, lambda E: E.dma_start(out=com["ones_bf"], in_=ones_d))
    p.inc("sync", "s_ld", 16, lambda E: E.dma_start(out=com["ones_f32"], in_=ones_d))
    p.inc("gpsimd", "s_ldg", 16, lambda E: E.dma_start(out=com["maskT"], in_=maskt_d))
    p.inc("sync", "s_ld", 16, lambda E: E.dma_start(out=com["cc_mask"], in_=ccm_d))
    p.barrier()

    wgt_q = nc.dram_tensor("wgt_q", [8, 128, 2816], BF16, kind="Internal").ap()
    wut_q = nc.dram_tensor("wut_q", [8, 128, 2816], BF16, kind="Internal").ap()
    p.new_sem("swq")
    p.new_sem("swst0")
    p.new_sem("swst1")
    reset()
    B = dict(com)
    B.update({
        "ps_row": com["ps_e"].rearrange("p a b -> p (a b)")[0:1, 0:512],
        "ps_tcol": com["ps_f"].rearrange("p a b -> p (a b)")[:, 0:4],
    })
    xln_fm = carve([128, 8, 512], BF16)
    B["g_fm"] = carve([128, 32, 512], BF16)
    B["v_sb"] = carve([128, 32, 1024], BF16)
    B["kt0"] = carve([128, 8, 128], BF16)
    B["kt1"] = carve([128, 8, 128], BF16)
    B["g20"] = carve([128, 512], BF16)
    B["g21"] = carve([128, 512], BF16)
    B["ssrow"] = carve([128, 512], F32)[0:1, :]
    B["tmp4"] = carve([128, 4], F32)
    B["r64"] = carve([128, 16], F32)
    Qw = {
        "xn": carve([128, 1024], F32),
        "ax": carve([128, 1024], F32),
        "qa": carve([128, 1024], BF16),
        "qb": carve([128, 1024], BF16),
        "sgn": carve([128, 1024], BF16),
        "sc": carve([128, 64], F32),
        "rs": carve([128, 64], F32),
        "hs": carve([128, 64], F32),
        "dst_tm": carve([128, 1024], BF16),
    }
    B["Qw"] = Qw
    B["wqc"] = carve([128, 8, 128], BF16)
    wstage2 = carve([128, 2, 1024], F32)
    wg4 = wg_d.rearrange("(r p) d -> p r d", p=128)
    wu4 = wu_d.rearrange("(r p) d -> p r d", p=128)

    extra = []
    _wq_state = {"i": 0}

    _wq_cks = {}

    def _mk_tile(src4, dst_q, r):
        def emit_tile():
            i = _wq_state["i"]
            _wq_state["i"] += 1
            sem = "swst%d" % (i % 2)
            if i >= 2:
                p.wait("gpsimd", "s_dve", _wq_cks[i - 2])
            ck_st = p.inc("gpsimd", sem, 16,
                          lambda E, src4=src4, r=r, i=i:
                          E.dma_start(out=wstage2[:, i % 2, :], in_=src4[:, r, :]))
            emit_wq_tile(p, B, None,
                         dst_q[:, :, r * 128:(r + 1) * 128]
                         .rearrange("kc p i -> p kc i"),
                         wstage2[:, i % 2, :], sem, ck_st)
            _wq_cks[i] = p.counts["s_dve"]
        return emit_tile

    for r in range(22):
        extra.append(_mk_tile(wg4, wgt_q, r))
    for r in range(22):
        extra.append(_mk_tile(wu4, wut_q, r))

    emit_l0_attn(p, x_sb, xln_fm, B, kt_d, v_d, extra_work=extra)
    p.barrier()

    reset()
    B = dict(com)

    def mkQ():
        return {
            "xn": carve([128, 1024], F32),
            "ax": carve([128, 1024], F32),
            "qa": carve([128, 1024], BF16),
            "qb": carve([128, 1024], BF16),
            "sgn": carve([128, 1024], BF16),
            "sc": carve([128, 64], F32),
            "rs": carve([128, 64], F32),
            "hs": carve([128, 64], F32),
            "x32": carve([128, 1024], F32),
            "dst_tm": carve([128, 1024], BF16),
        }
    B["Qv"] = mkQ()
    B["Qg"] = mkQ()
    B["xq_fm"] = carve([128, 8, 2048], BF16)
    B["wd_raw"] = carve([128, 8, 512], F32)
    B["wgT"] = carve([128, 8, 512], BF16)
    B["wuT"] = carve([128, 8, 512], BF16)
    B["wdT"] = carve([128, 4, 1024], BF16)
    B["sg"] = carve([128, 2, 512], F32)
    B["h32"] = carve([128, 2, 512], F32)
    B["hq_fm"] = carve([128, 4, 128], BF16)
    emit_l0_ffn(p, x_sb, B, wgt_q, wut_q, wd_d)
    p.barrier()

    reset()
    B = dict(com)
    B["Q"] = {"xn": carve([128, 1024], F32)}
    B["sq_bf"] = carve([128, 1024], BF16)
    B["xr"] = carve([128, 4, 8, 128], BF16)
    B["q_ch"] = carve([128, 4, 1024], BF16)
    B["q_ld0"] = carve([128, 1024], BF16)
    B["q_ld1"] = carve([128, 1024], BF16)
    B["kn_tm"] = carve([128, 16, 1024], BF16)
    B["v_tm"] = carve([128, 16, 1024], BF16)
    B["y_fm"] = carve([128, 8, 2048], BF16)
    B["wt0"] = carve([128, 512], BF16)
    B["wt1"] = carve([128, 512], BF16)
    B["st16"] = carve([128, 16], F32)
    B["st16b"] = carve([128, 16], F32)
    B["q_rot"] = carve([128, 8, 128], BF16)
    B["kn_rot"] = carve([128, 8, 128], BF16)
    B["scm0"] = carve([128, 128], BF16)
    B["scm1"] = carve([128, 128], BF16)
    B["ST32"] = carve([128, 8, 64], F32)
    B["STbf"] = carve([128, 8, 64], BF16)
    B["T_sb"] = carve([128, 8, 64], F32)
    B["tr_sb"] = B["T_sb"]
    emit_l1_attn(p, x_sb, B, wqkv_d, wo_d, skip_cc=skip_cc)
    p.barrier()

    reset()
    B = dict(com)
    xln_fm = carve([128, 8, 2048], BF16)
    B["h1g"] = carve([128, 8, 2048], BF16)
    B["w2q"] = carve([128, 8, 1024], BF16)
    B["route_tm"] = carve([128, 16, 8], F32)
    B["route_bf"] = carve([128, 8], BF16)
    B["route_fm"] = carve([128, 2048], BF16)[0:8, :]
    B["rwt_sb"] = carve([128, 8, 8], BF16)
    B["b2_sb"] = carve([128, 1024], BF16)[0:8, :]
    B["b1c"] = carve([128, 2, 32], F32)
    B["w1s0"] = carve([128, 8, 128], BF16)
    B["w1s1"] = carve([128, 8, 128], BF16)
    B["lg"] = carve([128, 8], F32)
    B["top8"] = carve([128, 8], F32)
    B["m1"] = carve([128, 1], F32)
    B["m2"] = carve([128, 1], F32)
    B["s1"] = carve([128, 1], F32)
    B["s2"] = carve([128, 1], F32)
    B["i1"] = carve([128, 8], F32)
    B["i2"] = carve([128, 8], F32)
    emit_ln(p, x_sb, xln_fm, B)
    emit_moe(p, x_sb, xln_fm, B, rwt_d, w1t_d, w2t_d, b1r_d, b2_d)

    p.barrier(("sync",))
    p.inc("sync", "s_out", 16,
          lambda E: E.dma_start(out=xo_d.rearrange("(c p) d -> p c d", p=128), in_=x_sb))
    p.wait("sync", "s_out")
    p.build()
    return nc


def _make_ccm(core):
    m = np.zeros((128, 8), np.float32)
    b, qq = core // 4, core % 4
    for j in range(8):
        if j // 4 == b and j % 4 < qq:
            m[:, j] = 1.0
    return m


def kernel(L0_attn_inputs, L0_attn_key_param_tokens, L0_attn_value_param_tokens,
           L0_ffn_w_gate, L0_ffn_w_up, L0_ffn_w_down,
           L1_attn_W_qkv, L1_attn_W_o,
           L1_ffn_router_w, L1_ffn_w1, L1_ffn_b1, L1_ffn_w2, L1_ffn_b2):
    g = lambda t: np.ascontiguousarray(np.asarray(t, np.float32))
    x = g(L0_attn_inputs)
    ins = {
        "kt": np.ascontiguousarray(g(L0_attn_key_param_tokens).T),
        "v": g(L0_attn_value_param_tokens),
        "wg": g(L0_ffn_w_gate),
        "wu": g(L0_ffn_w_up),
        "wd": g(L0_ffn_w_down),
        "wqkv": g(L1_attn_W_qkv),
        "wo": g(L1_attn_W_o),
        "rwt": np.ascontiguousarray(g(L1_ffn_router_w).T),
        "w1t": np.ascontiguousarray(np.transpose(g(L1_ffn_w1), (0, 2, 1))),
        "w2t": np.ascontiguousarray(np.transpose(g(L1_ffn_w2), (0, 2, 1))),
        "b1r": np.ascontiguousarray(g(L1_ffn_b1).reshape(8, 32, 128).transpose(0, 2, 1)),
        "b2": g(L1_ffn_b2),
        "eye": np.eye(128, dtype=np.float32),
        "ones": np.ones((128, 1), np.float32),
        "maskt": np.triu(np.ones((128, 128), np.float32)),
    }
    nc = build_fused(skip_cc=False)
    in_maps = []
    for c in range(8):
        b, q = c // 4, c % 4
        m = dict(ins)
        m["x"] = np.ascontiguousarray(x[b, q * 2048:(q + 1) * 2048, :])
        m["ccm"] = _make_ccm(c)
        in_maps.append(m)
    res = run_bass_kernel_spmd(nc, in_maps, core_ids=list(range(8)))
    out = np.zeros((2, 8192, 1024), np.float32)
    for c in range(8):
        b, q = c // 4, c % 4
        out[b, q * 2048:(q + 1) * 2048, :] = res.results[c]["xo"]
    return out
